# version 04
# baseline (speedup 1.0000x reference)
import numpy as np
import ml_dtypes
import concourse.bass as bass
import concourse.mybir as mybir
from concourse.bass_utils import run_bass_kernel_spmd

F32 = mybir.dt.float32
BF16 = mybir.dt.bfloat16
I32 = mybir.dt.int32
AF = mybir.ActivationFunctionType
ALU = mybir.AluOpType
BF = ml_dtypes.bfloat16

NCORES = 8
D = 1024
NCH = 8
SEQN = 8
BLOC = 256
T = 2048
NT = 4
TS = 512
NG = 16
H = 16
HD = 64

SM_BQ = 0
SM_BK = 8
SM_BPROJ = 16
SM_BIAS = 24
SM_GAMMA = 72
SM_BETA = 96
SM_EPS = 120
SM_N = 128

CO_ONES = 0
CO_A33 = 128
CO_MV33 = 256
CO_ESEL = 768
CO_BV = 1792
CO_N = 1800


class S:
    def __init__(self, h):
        self.h = h
        self.n = 0


class Prog:
    ENGINES = ("tensor", "vector", "scalar", "gpsimd", "sync")

    def __init__(self):
        self.streams = {e: [] for e in self.ENGINES}
        self.autosems = {}

    def wait(self, eng, sem, val):
        if val is None or val <= 0:
            return
        self.streams[eng].append(lambda e, s=sem, v=val: e.wait_ge(s.h, v))

    def op(self, eng, fn, inc=None, k=1):
        if inc is None:
            inc = self.autosems.get(eng)
            if inc is None:
                self.streams[eng].append(fn)
                return None
        inc.n += k
        tok = inc.n
        self.streams[eng].append(lambda e, f=fn, s=inc, kk=k: f(e).then_inc(s.h, kk))
        return tok

    def sw(self, eng):
        sem = self.autosems[eng]
        self.wait(eng, sem, sem.n)


class Ring:
    def __init__(self, p, sems, queue):
        self.p = p
        self.sems = sems
        self.queue = queue
        self.i = 0

    def issue(self, fn):
        r = self.i % len(self.sems)
        n = self.i // len(self.sems)
        self.i += 1
        sem = self.sems[r]
        if n > 0:
            self.p.wait(self.queue, sem, 16 * n)
        self.p.op(self.queue, fn, inc=sem, k=16)
        return (sem, 16 * (n + 1))

    def drain(self, eng):
        for r, sem in enumerate(self.sems):
            tot = (self.i - 1 - r) // len(self.sems) + 1 if r < self.i else 0
            if tot > 0:
                self.p.wait(eng, sem, 16 * tot)


def build_nc(taps=()):
    nc = bass.Bass("TRN2", target_bir_lowering=False)

    xt = nc.dram_tensor("xt", [D, T], F32, kind="ExternalInput").ap()
    wqkvT = nc.dram_tensor("wqkvT", [D, 3 * D], F32, kind="ExternalInput").ap()
    wprojT = nc.dram_tensor("wprojT", [D, D], F32, kind="ExternalInput").ap()
    qT = nc.dram_tensor("qT", [6, D, D], I32, kind="ExternalInput").ap()
    sclT = nc.dram_tensor("sclT", [6, 64, D], F32, kind="ExternalInput").ap()
    laT = nc.dram_tensor("laT", [6, 32, D], F32, kind="ExternalInput").ap()
    lbT = nc.dram_tensor("lbT", [6, 32, D], F32, kind="ExternalInput").ap()
    smalls = nc.dram_tensor("smalls", [128, SM_N], F32, kind="ExternalInput").ap()
    consts = nc.dram_tensor("consts", [128, CO_N], BF16, kind="ExternalInput").ap()
    out = nc.dram_tensor("o", [D, T], F32, kind="ExternalOutput").ap()
    tap_out = {}
    for tname in taps:
        tap_out[tname] = nc.dram_tensor("tap_" + tname, [D, T], F32,
                                        kind="ExternalOutput").ap()

    G = [nc.alloc_sbuf_tensor(f"big{i}", [128, NCH * T], BF16) for i in range(4)]
    WPOOL = nc.alloc_sbuf_tensor("wpool", [128, 2 * NCH * D], BF16)
    PL = nc.alloc_sbuf_tensor("pl", [128, 5 * T], BF16)
    OCHP = nc.alloc_sbuf_tensor("ochp", [128, 2 * T], F32)
    WSM = nc.alloc_sbuf_tensor("wsm", [128, 3 * D], BF16)
    BPP = nc.alloc_sbuf_tensor("bpp", [128, 8], F32)
    SMALLS = nc.alloc_sbuf_tensor("smalls_sb", [128, SM_N], F32)
    CONSTS = nc.alloc_sbuf_tensor("consts_sb", [128, CO_N], BF16)

    PS_A = nc.alloc_psum_tensor("psA", [128, T], F32)
    PS_B = nc.alloc_psum_tensor("psB", [128, T], F32)

    def pl(i):
        return PL[:, i * T:(i + 1) * T]
    WEFF = [WPOOL[:, 0:NCH * D], WPOOL[:, NCH * D:2 * NCH * D]]
    WVB = WPOOL[:, 0:NCH * D]
    WPB = WPOOL[:, NCH * D:2 * NCH * D]
    SMF0 = OCHP[:, 0:T]
    SMF1 = OCHP[:, T:2 * T]
    OCH = [OCHP[:, 0:T], OCHP[:, T:2 * T]]
    QBF = WSM[:, 0:D]
    TW1 = PL[:, 3 * T:3 * T + D]
    SCLB = WSM[0:64, 2 * D:3 * D]
    LAB = WSM[64:96, 2 * D:3 * D]
    LBB = WSM[64:96, D:2 * D]
    WQC = [WSM[:, 0:D], WSM[:, D:2 * D]]
    VT = PL[:, 3 * T:3 * T + 2 * D]

    p = Prog()
    import contextlib
    with contextlib.ExitStack() as stack:
        hL = [stack.enter_context(nc.semaphore(f"sL{i}")) for i in range(8)]
        hS = [stack.enter_context(nc.semaphore(f"sS{i}")) for i in range(2)]
        hPE = stack.enter_context(nc.semaphore("sPE"))
        hDV = stack.enter_context(nc.semaphore("sDV"))
        hAC = stack.enter_context(nc.semaphore("sAC"))
        block = stack.enter_context(nc.Block())
        sPE, sDV, sAC = S(hPE), S(hDV), S(hAC)
        p.autosems = {"vector": sDV, "scalar": sAC}
        LD = Ring(p, [S(h) for h in hL], "gpsimd")
        STR = Ring(p, [S(h) for h in hS], "sync")

        def waitt(eng, tok):
            if tok is not None:
                p.wait(eng, tok[0], tok[1])
        eps_col = SMALLS[:, SM_EPS:SM_EPS + 1]
        ones = CONSTS[:, CO_ONES:CO_ONES + 128]
        x_src = xt.rearrange("(c p) t -> p c t", p=128)

        tok_smalls = LD.issue(lambda e: e.dma_start(SMALLS[:, :], smalls))
        tok_consts = LD.issue(lambda e: e.dma_start(CONSTS[:, :], consts))
        waitt("tensor", tok_consts)
        waitt("scalar", tok_smalls)
        waitt("vector", tok_smalls)
        waitt("vector", tok_consts)
        waitt("tensor", tok_smalls)

        slot_free_pe = [None, None]
        dv_sq = None
        pe_last = None
        for c in range(NCH):
            sl = c % 2
            p.wait("gpsimd", sPE, slot_free_pe[sl])
            t_x = LD.issue(lambda e, c=c, sl=sl: e.dma_start(
                pl(sl), x_src[:, c, :]))
            waitt("vector", t_x)
            p.wait("vector", sPE, pe_last)
            dv_sq = p.op("vector", lambda e, sl=sl: e.tensor_tensor(
                pl(2), pl(sl), pl(sl), ALU.mult), inc=sDV)
            p.wait("tensor", sDV, dv_sq)
            for ts in range(NT):
                p.op("tensor", lambda e, sl=sl, ts=ts, c=c: e.matmul(
                    PS_A[:, ts * TS:(ts + 1) * TS], ones,
                    pl(sl)[:, ts * TS:(ts + 1) * TS],
                    start=(c == 0), stop=(c == NCH - 1), skip_group_check=True))
            for ts in range(NT):
                pe_last = p.op("tensor", lambda e, ts=ts, c=c: e.matmul(
                    PS_B[:, ts * TS:(ts + 1) * TS], ones,
                    pl(2)[:, ts * TS:(ts + 1) * TS],
                    start=(c == 0), stop=(c == NCH - 1), skip_group_check=True),
                    inc=sPE)
            slot_free_pe[sl] = pe_last

        def stats_tail(pe_stats):
            p.wait("scalar", sPE, pe_stats)
            ac_msq = p.op("scalar", lambda e: e.activation(
                SMF0, PS_A[:, :], AF.Square, scale=1.0 / D), inc=sAC)
            p.wait("vector", sAC, ac_msq)
            p.op("vector", lambda e: e.scalar_tensor_tensor(
                SMF1, PS_B[:, :], 1.0 / D, SMF0, ALU.mult, ALU.subtract))
            dv_var = p.op("vector", lambda e: e.tensor_scalar(
                pl(3), PS_A[:, :], 1.0 / D, None, ALU.mult), inc=sDV)
            p.wait("scalar", sDV, dv_var)
            p.sw("scalar")
            p.op("scalar", lambda e: e.activation(SMF0, SMF1, AF.Ln, bias=eps_col))
            p.sw("scalar")
            ac_rs = p.op("scalar", lambda e: e.activation(
                pl(4), SMF0, AF.Exp, scale=-0.5), inc=sAC)
            return dv_var, ac_rs

        ln0_stats_pe = pe_last
        ln0_psum_free, ln0_rs = stats_tail(pe_last)
        p.wait("vector", sAC, ln0_rs)
        p.wait("gpsimd", sPE, ln0_stats_pe)
        slot_free_dv = [None, None]
        ln0_out = None
        for c in range(NCH):
            sl = c % 2
            p.wait("gpsimd", sDV, slot_free_dv[sl])
            t_x = LD.issue(lambda e, c=c, sl=sl: e.dma_start(
                pl(sl), x_src[:, c, :]))
            waitt("vector", t_x)
            p.sw("vector")
            slot_free_dv[sl] = p.op("vector", lambda e, sl=sl: e.tensor_tensor(
                pl(2), pl(sl), pl(3), ALU.subtract), inc=sDV)
            p.sw("vector")
            ln0_out = p.op("vector", lambda e, c=c: e.tensor_tensor(
                G[0][:, c * T:(c + 1) * T], pl(2), pl(4), ALU.mult), inc=sDV)

        def emit_ln_gelu(src_big, dst_big, blk):
            pe_last = None
            dv_sq = None
            for c in range(NCH):
                p.wait("vector", sPE, pe_last)
                p.sw("vector")
                dv_sq = p.op("vector", lambda e, c=c: e.tensor_tensor(
                    pl(0), src_big[:, c * T:(c + 1) * T],
                    src_big[:, c * T:(c + 1) * T], ALU.mult), inc=sDV)
                p.wait("tensor", sDV, dv_sq)
                for ts in range(NT):
                    p.op("tensor", lambda e, c=c, ts=ts, src_big=src_big: e.matmul(
                        PS_A[:, ts * TS:(ts + 1) * TS], ones,
                        src_big[:, c * T + ts * TS: c * T + (ts + 1) * TS],
                        start=(c == 0), stop=(c == NCH - 1), skip_group_check=True))
                for ts in range(NT):
                    pe_last = p.op("tensor", lambda e, ts=ts, c=c: e.matmul(
                        PS_B[:, ts * TS:(ts + 1) * TS], ones,
                        pl(0)[:, ts * TS:(ts + 1) * TS],
                        start=(c == 0), stop=(c == NCH - 1), skip_group_check=True),
                        inc=sPE)
            dv_var, ac_rs = stats_tail(pe_last)
            p.wait("vector", sAC, ac_rs)
            prev_gelu = None
            for c in range(NCH):
                p.sw("vector")
                p.op("vector", lambda e, c=c: e.tensor_tensor(
                    pl(1), src_big[:, c * T:(c + 1) * T], pl(3), ALU.subtract))
                p.wait("vector", sAC, prev_gelu)
                p.sw("vector")
                tkv = p.op("vector", lambda e: e.tensor_tensor(
                    pl(2), pl(1), pl(4), ALU.mult), inc=sDV)
                p.wait("scalar", sDV, tkv)
                gcol = SMALLS[:, SM_GAMMA + blk * 8 + c: SM_GAMMA + blk * 8 + c + 1]
                bcol = SMALLS[:, SM_BETA + blk * 8 + c: SM_BETA + blk * 8 + c + 1]
                prev_gelu = p.op("scalar", lambda e, c=c, g_=gcol, b_=bcol: e.activation(
                    dst_big[:, c * T:(c + 1) * T], pl(2), AF.Gelu,
                    scale=g_, bias=b_), inc=sAC)
            return dv_var, prev_gelu

        wprep_dv_done = [None, None]
        wprep_pe_last = None
        wprep_last_dv1 = None

        def emit_wprep(j, par, psb_free=None):
            nonlocal wprep_pe_last, wprep_last_dv1
            p.wait("gpsimd", sPE, wprep_pe_last)
            t_sc = LD.issue(lambda e, j=j: e.dma_start(SCLB, sclT[j]))
            t_la = LD.issue(lambda e, j=j: e.dma_start(LAB, laT[j]))
            t_lb = LD.issue(lambda e, j=j: e.dma_start(LBB, lbT[j]))
            dv1 = wprep_last_dv1
            dv2 = psb_free
            waitt("tensor", t_sc)
            waitt("tensor", t_la)
            waitt("tensor", t_lb)
            for c in range(NCH):
                p.wait("gpsimd", sDV, dv1)
                t_q = LD.issue(lambda e, j=j, c=c: e.dma_start(
                    QBF, qT[j, c * 128:(c + 1) * 128, :]))
                p.wait("tensor", sDV, dv2)
                esel = CONSTS[0:64, CO_ESEL + c * 128: CO_ESEL + (c + 1) * 128]
                for half in range(2):
                    p.op("tensor", lambda e, half=half, esel=esel: e.matmul(
                        PS_B[:, half * TS:(half + 1) * TS], esel,
                        SCLB[:, half * TS:(half + 1) * TS], start=True, stop=True))
                pe_tok = None
                for half in range(2):
                    pe_tok = p.op("tensor", lambda e, c=c, half=half: e.matmul(
                        PS_B[:, 1024 + half * TS: 1024 + (half + 1) * TS],
                        LAB[:, c * 128:(c + 1) * 128],
                        LBB[:, half * TS:(half + 1) * TS], start=True, stop=True),
                        inc=sPE)
                wprep_pe_last = pe_tok
                p.wait("vector", sPE, pe_tok)
                waitt("vector", t_q)
                p.sw("vector")
                dv1 = p.op("vector", lambda e: e.scalar_tensor_tensor(
                    TW1, QBF, 7.5, PS_B[:, 0:1024], ALU.subtract, ALU.mult), inc=sDV)
                p.sw("vector")
                dv2 = p.op("vector", lambda e, par=par, c=c: e.scalar_tensor_tensor(
                    WEFF[par][:, c * D:(c + 1) * D], TW1, 1.0 / 7.5,
                    PS_B[:, 1024:2048], ALU.mult, ALU.add), inc=sDV)
            wprep_last_dv1 = dv1
            return dv2

        def emit_base_mm(par, in_big, evac):
            p.wait("tensor", sDV, wprep_dv_done[par])
            evac_tok = {}
            for m in range(NCH):
                for ts in range(NT):
                    if (m - 1, ts) in evac_tok:
                        p.wait("tensor", sDV, evac_tok[(m - 1, ts)])
                    pe_tok = None
                    for k in range(NCH):
                        pe_tok = p.op("tensor", lambda e, m=m, ts=ts, k=k, par=par,
                                      in_big=in_big: e.matmul(
                            PS_A[:, ts * TS:(ts + 1) * TS],
                            WEFF[par][:, k * D + m * 128: k * D + (m + 1) * 128],
                            in_big[:, k * T + ts * TS: k * T + (ts + 1) * TS],
                            start=(k == 0), stop=(k == NCH - 1)),
                            inc=(sPE if k == NCH - 1 else None))
                    evac_tok[(m, ts)] = evac(m, ts, pe_tok)
            return evac_tok

        h3_done = None
        for blk in range(3):
            j0, j1 = 2 * blk, 2 * blk + 1
            par0, par1 = j0 % 2, j1 % 2
            wprep_dv_done[par0] = emit_wprep(
                j0, par0, psb_free=(ln0_psum_free if blk == 0 else None))
            in_even = G[0] if blk == 0 else (G[3] if blk == 1 else G[1])
            ybuf = G[1] if blk < 2 else G[3]
            if blk == 0:
                p.wait("tensor", sDV, ln0_out)

            def evac_even(m, ts, pe_tok, j=j0, ybuf=ybuf):
                p.wait("vector", sPE, pe_tok)
                bcol = SMALLS[:, SM_BIAS + j * 8 + m: SM_BIAS + j * 8 + m + 1]
                return p.op("vector", lambda e, m=m, ts=ts, bcol=bcol, ybuf=ybuf:
                            e.tensor_scalar(
                                ybuf[:, m * T + ts * TS: m * T + (ts + 1) * TS],
                                PS_A[:, ts * TS:(ts + 1) * TS], bcol, None, ALU.add),
                            inc=sDV)
            emit_base_mm(par0, in_even, evac_even)

            wprep_dv_done[par1] = emit_wprep(j1, par1)
            ln_psum_free, gelu_out = emit_ln_gelu(ybuf, G[2], blk)

            hin = in_even
            hout = G[3] if blk == 0 else (G[1] if blk == 1 else G[3])

            def evac_odd(m, ts, pe_tok, j=j1, hin=hin, hout=hout):
                p.wait("vector", sPE, pe_tok)
                bcol = SMALLS[:, SM_BIAS + j * 8 + m: SM_BIAS + j * 8 + m + 1]
                return p.op("vector", lambda e, m=m, ts=ts, bcol=bcol, hin=hin, hout=hout:
                            e.scalar_tensor_tensor(
                                hout[:, m * T + ts * TS: m * T + (ts + 1) * TS],
                                PS_A[:, ts * TS:(ts + 1) * TS], bcol,
                                hin[:, m * T + ts * TS: m * T + (ts + 1) * TS],
                                ALU.add, ALU.add), inc=sDV)
            p.wait("tensor", sAC, gelu_out)
            p.wait("tensor", sDV, ln_psum_free)
            ev = emit_base_mm(par1, G[2], evac_odd)
            h3_done = ev[(NCH - 1, NT - 1)]

        p.wait("scalar", sDV, h3_done)
        silu_ac = None
        for c in range(NCH):
            silu_ac = p.op("scalar", lambda e, c=c: e.activation(
                G[1][:, c * T:(c + 1) * T], G[3][:, c * T:(c + 1) * T], AF.Silu),
                inc=sAC)

        p.wait("gpsimd", sDV, h3_done)
        wq_src = wqkvT.rearrange("(c p) o -> p c o", p=128)
        tok_wv = LD.issue(lambda e: e.dma_start(
            WVB.rearrange("p (c o) -> p c o", c=NCH),
            wq_src[:, :, 2 * D:3 * D]))
        tok_wp = LD.issue(lambda e: e.dma_start(
            WPB.rearrange("p (c o) -> p c o", c=NCH),
            wprojT.rearrange("(c p) o -> p c o", p=128)))

        wqc_tok = [None, None]
        qk_evac = {}
        qk_last = None
        for m in range(16):
            par = m % 2
            p.wait("gpsimd", sPE, wqc_tok[par])
            t_w = LD.issue(lambda e, m=m, par=par: e.dma_start(
                WQC[par].rearrange("p (c o) -> p c o", c=NCH),
                wq_src[:, :, m * 128:(m + 1) * 128]))
            waitt("tensor", t_w)
            dst = G[2] if m < 8 else G[3]
            mm_ = m % 8
            bias_off = SM_BQ if m < 8 else SM_BK
            last_pe_m = None
            for ts in range(NT):
                if (m - 2, ts) in qk_evac:
                    p.wait("tensor", sAC, qk_evac[(m - 2, ts)])
                ps = PS_A if (m % 2 == 0) else PS_B
                pe_tok = None
                for k in range(NCH):
                    pe_tok = p.op("tensor", lambda e, m=m, ts=ts, k=k, ps=ps: e.matmul(
                        ps[:, ts * TS:(ts + 1) * TS],
                        WQC[m % 2][:, k * 128:(k + 1) * 128],
                        G[0][:, k * T + ts * TS: k * T + (ts + 1) * TS],
                        start=(k == 0), stop=(k == NCH - 1)),
                        inc=(sPE if k == NCH - 1 else None))
                last_pe_m = pe_tok
                p.wait("scalar", sPE, pe_tok)
                bcol = SMALLS[:, bias_off + mm_: bias_off + mm_ + 1]
                qk_evac[(m, ts)] = p.op(
                    "scalar", lambda e, dst=dst, mm_=mm_, ts=ts, ps=ps, bcol=bcol:
                    e.activation(
                        dst[:, mm_ * T + ts * TS: mm_ * T + (ts + 1) * TS],
                        ps[:, ts * TS:(ts + 1) * TS], AF.Identity, bias=bcol), inc=sAC)
                qk_last = qk_evac[(m, ts)]
            wqc_tok[par] = last_pe_m

        waitt("tensor", tok_wp)
        p.wait("tensor", sAC, qk_last)
        tk_bp = None
        for m in range(NCH):
            for k in range(NCH):
                tk_bp = p.op("tensor", lambda e, m=m, k=k: e.matmul(
                    PS_A[:, 1536 + m: 1537 + m],
                    WPB[:, k * D + m * 128: k * D + (m + 1) * 128],
                    CONSTS[:, CO_BV + k: CO_BV + k + 1],
                    start=(k == 0), stop=(k == NCH - 1), skip_group_check=True),
                    inc=(sPE if k == NCH - 1 else None))
        p.wait("vector", sPE, tk_bp)
        bp_dv = p.op("vector", lambda e: e.tensor_tensor(
            BPP[:, :], PS_A[:, 1536:1544], SMALLS[:, SM_BPROJ:SM_BPROJ + 8], ALU.add),
            inc=sDV)

        a33 = CONSTS[0:33, CO_A33:CO_A33 + 128]
        ET, RT, ENT = pl(0), pl(1), pl(2)
        prev_av_evac = None
        prev_recipB = None
        prev_en = None
        prev_av_pe = None
        v_ev = [None, None]
        av_read = [None, None]
        waitt("tensor", tok_wv)
        p.wait("tensor", sDV, bp_dv)
        ao_last = None
        for g in range(NG):
            sl = g % 2
            p.wait("tensor", sDV, v_ev[sl] if g >= 2 else None)
            if g >= 1:
                p.wait("tensor", sDV, v_ev[1 - sl])
            for half in range(2):
                pe_tok = None
                for k in range(NCH):
                    pe_tok = p.op("tensor", lambda e, g=g, half=half, k=k: e.matmul(
                        PS_B[:, half * TS:(half + 1) * TS],
                        G[0][:, k * T + g * 128: k * T + (g + 1) * 128],
                        WVB[:, k * D + half * TS: k * D + (half + 1) * TS],
                        start=(k == 0), stop=(k == NCH - 1)),
                        inc=(sPE if k == NCH - 1 else None))
                p.wait("vector", sPE, pe_tok)
                if g >= 2 and half == 0:
                    p.wait("vector", sPE, av_read[sl])
                v_ev[sl] = p.op("vector", lambda e, sl=sl, half=half: e.tensor_copy(
                    VT[:, sl * D + half * TS: sl * D + (half + 1) * TS],
                    PS_B[:, half * TS:(half + 1) * TS]), inc=sDV)
            p.wait("tensor", sAC, prev_av_evac)
            for ts in range(NT):
                p.op("tensor", lambda e, ts=ts: e.matmul(
                    PS_A[:, ts * TS:(ts + 1) * TS], a33,
                    CONSTS[0:33, CO_MV33: CO_MV33 + TS],
                    start=True, stop=False, skip_group_check=True))
            tk_qk = None
            for h in range(H):
                c = h // 2
                r0 = 64 * (h % 2)
                tk_qk = p.op("tensor", lambda e, g=g, h=h, c=c, r0=r0: e.matmul(
                    PS_A[:, h * 128:(h + 1) * 128],
                    G[3][r0:r0 + 64, c * T + g * 128: c * T + (g + 1) * 128],
                    G[2][r0:r0 + 64, c * T + g * 128: c * T + (g + 1) * 128],
                    start=False, stop=True, skip_group_check=True),
                    inc=(sPE if h == H - 1 else None))
            p.wait("scalar", sPE, tk_qk)
            p.wait("scalar", sDV, prev_en)
            tk_exp = None
            for half in range(2):
                tk_exp = p.op("scalar", lambda e, half=half: e.activation(
                    ET[:, half * 1024:(half + 1) * 1024],
                    PS_A[:, half * 1024:(half + 1) * 1024], AF.Exp, scale=0.125),
                    inc=(sAC if half == 1 else None))
            p.wait("tensor", sAC, tk_exp)
            tk_recip = None
            for half in range(2):
                p.wait("tensor", sDV, prev_recipB if half == 0 else tk_recip)
                tk_den = None
                for q4 in range(2):
                    tk_den = p.op("tensor", lambda e, half=half, q4=q4: e.matmul(
                        PS_B[:, 1024 + q4 * TS: 1024 + (q4 + 1) * TS], ones,
                        ET[:, half * 1024 + q4 * TS: half * 1024 + (q4 + 1) * TS],
                        start=True, stop=True),
                        inc=(sPE if q4 == 1 else None))
                p.wait("vector", sPE, tk_den)
                tk_recip = p.op("vector", lambda e, half=half: e.reciprocal(
                    RT[:, half * 1024:(half + 1) * 1024], PS_B[:, 1024:2048]), inc=sDV)
            prev_recipB = tk_recip
            p.wait("vector", sPE, prev_av_pe)
            p.sw("vector")
            prev_en = p.op("vector", lambda e: e.tensor_tensor(
                ENT[:, :], ET[:, :], RT[:, :], ALU.mult), inc=sDV)
            p.wait("tensor", sDV, prev_en)
            p.wait("tensor", sDV, v_ev[sl])
            tk_av = None
            for h in range(H):
                c = h // 2
                r0 = 64 * (h % 2)
                tk_av = p.op("tensor", lambda e, g=g, h=h, c=c, r0=r0, sl=sl: e.matmul(
                    PS_A[r0:r0 + 64, c * 128:(c + 1) * 128],
                    VT[:, sl * D + h * HD: sl * D + (h + 1) * HD],
                    ENT[:, h * 128:(h + 1) * 128],
                    start=True, stop=True, skip_group_check=True),
                    inc=(sPE if h == H - 1 else None))
            av_read[sl] = tk_av
            prev_av_pe = tk_av
            p.wait("scalar", sPE, tk_av)
            prev_av_evac = p.op("scalar", lambda e, g=g: e.activation(
                G[0][:, :].rearrange("p (c t) -> p c t", c=NCH)[:, :, g * 128:(g + 1) * 128],
                PS_A[:, 0:1024].rearrange("p (c t) -> p c t", c=NCH), AF.Copy), inc=sAC)
            ao_last = prev_av_evac

        p.wait("tensor", sAC, ao_last)
        s_evac = {}
        for m in range(NCH):
            for ts in range(NT):
                if (m - 2, ts) in s_evac:
                    p.wait("tensor", sDV, s_evac[(m - 2, ts)])
                ps = PS_A if (m % 2 == 0) else PS_B
                psl = ps[:, ts * TS:(ts + 1) * TS]
                pe_tok = None
                for k in range(NCH):
                    pe_tok = p.op("tensor", lambda e, m=m, ts=ts, k=k, psl=psl: e.matmul(
                        psl,
                        WPB[:, k * D + m * 128: k * D + (m + 1) * 128],
                        G[0][:, k * T + ts * TS: k * T + (ts + 1) * TS],
                        start=(k == 0), stop=(k == NCH - 1)),
                        inc=(sPE if k == NCH - 1 else None))
                p.wait("vector", sPE, pe_tok)
                s_evac[(m, ts)] = p.op("vector", lambda e, m=m, ts=ts, psl=psl:
                                       e.tensor_scalar(
                                           G[2][:, m * T + ts * TS: m * T + (ts + 1) * TS],
                                           psl, BPP[:, m:m + 1], None, ALU.add), inc=sDV)

        p.wait("gpsimd", sAC, ao_last)
        xch_tok = [None, None]
        och_free = [None, None]
        st_total = None
        for c in range(NCH):
            par = c % 2
            p.wait("gpsimd", sDV, xch_tok[par])
            t_x = LD.issue(lambda e, c=c, par=par: e.dma_start(
                pl(par), x_src[:, c, :]))
            p.wait("vector", sDV, s_evac[(c, NT - 1)])
            p.wait("vector", sAC, silu_ac)
            waitt("vector", t_x)
            waitt("vector", och_free[par])
            p.sw("vector")
            p.op("vector", lambda e, c=c: e.tensor_tensor(
                pl(2), G[2][:, c * T:(c + 1) * T], G[1][:, c * T:(c + 1) * T], ALU.mult))
            p.sw("vector")
            xch_tok[par] = p.op("vector", lambda e, c=c, par=par: e.tensor_tensor(
                OCH[par], pl(par), pl(2), ALU.add), inc=sDV)
            p.wait("sync", sDV, xch_tok[par])
            tk_st = STR.issue(lambda e, c=c, par=par: e.dma_start(
                out[c * 128:(c + 1) * 128, :], OCH[par]))
            och_free[par] = tk_st
            st_total = tk_st

        for tname, ap in tap_out.items():
            idx = {"xn": 0, "g": 1, "q": 2, "k": 3, "ao": 0, "s": 2, "h3": 3,
                   "h1": 3, "y0": 1, "gelu": 2, "h2": 1}[tname]
            for c in range(NCH):
                tkv = p.op("vector", lambda e, c=c, idx=idx: e.tensor_copy(
                    OCH[0], G[idx][:, c * T:(c + 1) * T]), inc=sDV)
                p.wait("sync", sDV, tkv)
                tk = STR.issue(lambda e, c=c, ap=ap: e.dma_start(
                    ap[c * 128:(c + 1) * 128, :], OCH[0]))
                waitt("vector", tk)
                st_total = tk

        STR.drain("gpsimd")

        with nc.allow_low_precision(reason="bf16 compute sanctioned by problem tolerance"):
            @block.tensor
            def _(eng):
                for f in p.streams["tensor"]:
                    f(eng)

            @block.vector
            def _(eng):
                for f in p.streams["vector"]:
                    f(eng)

            @block.scalar
            def _(eng):
                for f in p.streams["scalar"]:
                    f(eng)

            @block.gpsimd
            def _(eng):
                for f in p.streams["gpsimd"]:
                    f(eng)

            @block.sync
            def _(eng):
                for f in p.streams["sync"]:
                    f(eng)

    return nc


def _host_pack(inputs):
    x = np.asarray(inputs["L0_attn_x"], dtype=np.float32)
    Wqkv = np.asarray(inputs["L0_attn_Wqkv"], dtype=np.float32)
    bqkv = np.asarray(inputs["L0_attn_bqkv"], dtype=np.float32)
    Wproj = np.asarray(inputs["L0_attn_Wproj"], dtype=np.float32)
    bproj = np.asarray(inputs["L0_attn_bproj"], dtype=np.float32)
    qw = np.asarray(inputs["L0_ffn_qweights"], dtype=np.int32)
    scl = np.asarray(inputs["L0_ffn_scales"], dtype=np.float32)
    bias = np.asarray(inputs["L0_ffn_biases"], dtype=np.float32)
    la = np.asarray(inputs["L0_ffn_lora_a"], dtype=np.float32)
    lb = np.asarray(inputs["L0_ffn_lora_b"], dtype=np.float32)
    gam = np.asarray(inputs["L0_ffn_ln_gamma"], dtype=np.float32)
    bet = np.asarray(inputs["L0_ffn_ln_beta"], dtype=np.float32)

    shared = {}
    shared["wqkvT"] = np.ascontiguousarray(Wqkv.T)
    shared["wprojT"] = np.ascontiguousarray(Wproj.T)
    shared["qT"] = np.ascontiguousarray(qw.transpose(0, 2, 1))
    shared["sclT"] = np.ascontiguousarray(scl.transpose(0, 2, 1))
    shared["laT"] = np.ascontiguousarray(la)
    shared["lbT"] = np.ascontiguousarray(lb.transpose(0, 2, 1))

    sm = np.zeros((128, SM_N), dtype=np.float32)
    sm[:, SM_BQ:SM_BQ + 8] = bqkv[0:1024].reshape(8, 128).T
    sm[:, SM_BK:SM_BK + 8] = bqkv[1024:2048].reshape(8, 128).T
    sm[:, SM_BPROJ:SM_BPROJ + 8] = bproj.reshape(8, 128).T
    for j in range(6):
        sm[:, SM_BIAS + j * 8: SM_BIAS + (j + 1) * 8] = bias[j].reshape(8, 128).T
    for blk in range(3):
        sm[:, SM_GAMMA + blk * 8: SM_GAMMA + (blk + 1) * 8] = gam[blk].reshape(8, 128).T
        sm[:, SM_BETA + blk * 8: SM_BETA + (blk + 1) * 8] = bet[blk].reshape(8, 128).T
    sm[:, SM_EPS] = 1e-5
    shared["smalls"] = sm

    co = np.zeros((128, CO_N), dtype=BF)
    co[:, CO_ONES:CO_ONES + 128] = 1.0
    A33 = np.zeros((33, 128), dtype=np.float32)
    P33 = np.zeros((33, 128), dtype=np.float32)
    r = np.arange(128)
    A33[0, :] = -128.0
    P33[0, :] = 128.0
    for b in range(16):
        inb = (r // 8 == b)
        A33[1 + 2 * b, :] = 128.0 * inb
        P33[1 + 2 * b, :] = 128.0 * inb
        A33[2 + 2 * b, :] = -128.0 * (inb & (r % 8 >= 4))
        P33[2 + 2 * b, :] = 128.0 * (inb & (r % 8 < 4))
    co[0:33, CO_A33:CO_A33 + 128] = A33.astype(BF)
    co[0:33, CO_MV33:CO_MV33 + 512] = np.tile(P33, (1, 4)).astype(BF)
    for c in range(NCH):
        E = np.zeros((64, 128), dtype=np.float32)
        d = np.arange(128)
        E[(8 * c + d // 16), d] = 1.0
        co[0:64, CO_ESEL + c * 128: CO_ESEL + (c + 1) * 128] = E.astype(BF)
    co[:, CO_BV:CO_BV + 8] = bqkv[2048:3072].reshape(8, 128).T.astype(BF)
    shared["consts"] = co

    per_core = []
    for core in range(NCORES):
        xs = x[:, core * BLOC:(core + 1) * BLOC, :]
        xtc = np.ascontiguousarray(xs.transpose(2, 1, 0).reshape(D, T))
        per_core.append({"xt": xtc})
    return shared, per_core


def kernel(**inputs):
    shared, per_core = _host_pack(inputs)
    nc = build_nc()
    in_maps = [{**shared, **pc} for pc in per_core]
    res = run_bass_kernel_spmd(nc, in_maps, core_ids=list(range(NCORES)))
    outs = []
    for core in range(NCORES):
        o = np.asarray(res.results[core]["o"])
        outs.append(o.reshape(D, BLOC, SEQN).transpose(2, 1, 0))
    return np.concatenate(outs, axis=1).astype(np.float32)


# version 05
# speedup vs baseline: 1.0001x; 1.0001x over previous
import numpy as np
import ml_dtypes
import concourse.bass as bass
import concourse.mybir as mybir
from concourse.bass_utils import run_bass_kernel_spmd

F32 = mybir.dt.float32
BF16 = mybir.dt.bfloat16
I32 = mybir.dt.int32
AF = mybir.ActivationFunctionType
ALU = mybir.AluOpType
BF = ml_dtypes.bfloat16

NCORES = 8
D = 1024
NCH = 8
SEQN = 8
BLOC = 256
T = 2048
NT = 4
TS = 512
NG = 16
H = 16
HD = 64

SM_BQ = 0
SM_BK = 8
SM_BPROJ = 16
SM_BIAS = 24
SM_GAMMA = 72
SM_BETA = 96
SM_EPS = 120
SM_BV = 121
SM_N = 136

CO_ONES = 0
CO_A33 = 128
CO_MV33 = 256
CO_ESEL = 768
CO_BV = 1792
CO_N = 1800


class S:
    def __init__(self, h):
        self.h = h
        self.n = 0


class Prog:
    ENGINES = ("tensor", "vector", "scalar", "gpsimd", "sync")

    def __init__(self):
        self.streams = {e: [] for e in self.ENGINES}
        self.autosems = {}

    def wait(self, eng, sem, val):
        if val is None or val <= 0:
            return
        self.streams[eng].append(lambda e, s=sem, v=val: e.wait_ge(s.h, v))

    def op(self, eng, fn, inc=None, k=1):
        if inc is None:
            inc = self.autosems.get(eng)
            if inc is None:
                self.streams[eng].append(fn)
                return None
        inc.n += k
        tok = inc.n
        self.streams[eng].append(lambda e, f=fn, s=inc, kk=k: f(e).then_inc(s.h, kk))
        return tok

    def sw(self, eng):
        sem = self.autosems[eng]
        self.wait(eng, sem, sem.n)


class Ring:
    def __init__(self, p, sems, queue):
        self.p = p
        self.sems = sems
        self.queue = queue
        self.i = 0

    def issue(self, fn):
        r = self.i % len(self.sems)
        n = self.i // len(self.sems)
        self.i += 1
        sem = self.sems[r]
        if n > 0:
            self.p.wait(self.queue, sem, 16 * n)
        self.p.op(self.queue, fn, inc=sem, k=16)
        return (sem, 16 * (n + 1))

    def drain(self, eng):
        for r, sem in enumerate(self.sems):
            tot = (self.i - 1 - r) // len(self.sems) + 1 if r < self.i else 0
            if tot > 0:
                self.p.wait(eng, sem, 16 * tot)


def build_nc(taps=()):
    nc = bass.Bass("TRN2", target_bir_lowering=False)

    xt = nc.dram_tensor("xt", [D, T], F32, kind="ExternalInput").ap()
    wqkvT = nc.dram_tensor("wqkvT", [D, 3 * D], F32, kind="ExternalInput").ap()
    wprojT = nc.dram_tensor("wprojT", [D, D], F32, kind="ExternalInput").ap()
    qT = nc.dram_tensor("qT", [6, D, D], I32, kind="ExternalInput").ap()
    sclT = nc.dram_tensor("sclT", [6, 64, D], F32, kind="ExternalInput").ap()
    laT = nc.dram_tensor("laT", [6, 32, D], F32, kind="ExternalInput").ap()
    lbT = nc.dram_tensor("lbT", [6, 32, D], F32, kind="ExternalInput").ap()
    smalls = nc.dram_tensor("smalls", [128, SM_N], F32, kind="ExternalInput").ap()
    consts = nc.dram_tensor("consts", [128, CO_N], BF16, kind="ExternalInput").ap()
    out = nc.dram_tensor("o", [D, T], F32, kind="ExternalOutput").ap()
    tap_out = {}
    for tname in taps:
        tap_out[tname] = nc.dram_tensor("tap_" + tname, [D, T], F32,
                                        kind="ExternalOutput").ap()

    G = [nc.alloc_sbuf_tensor(f"big{i}", [128, NCH * T], BF16) for i in range(4)]
    WPOOL = nc.alloc_sbuf_tensor("wpool", [128, 2 * NCH * D], BF16)
    PL = nc.alloc_sbuf_tensor("pl", [128, 5 * T], BF16)
    OCHP = nc.alloc_sbuf_tensor("ochp", [128, 2 * T], F32)
    WSM = nc.alloc_sbuf_tensor("wsm", [128, 3 * D], BF16)
    SMALLS = nc.alloc_sbuf_tensor("smalls_sb", [128, SM_N], F32)
    CONSTS = nc.alloc_sbuf_tensor("consts_sb", [128, CO_N], BF16)

    PS_A = nc.alloc_psum_tensor("psA", [128, T], F32)
    PS_B = nc.alloc_psum_tensor("psB", [128, T], F32)

    def pl(i):
        return PL[:, i * T:(i + 1) * T]
    WEFF = [WPOOL[:, 0:NCH * D], WPOOL[:, NCH * D:2 * NCH * D]]
    WVB = WPOOL[:, 0:NCH * D]
    WPB = WPOOL[:, NCH * D:2 * NCH * D]
    SMF0 = OCHP[:, 0:T]
    SMF1 = OCHP[:, T:2 * T]
    OCH = [OCHP[:, 0:T], OCHP[:, T:2 * T]]
    QBF = WSM[:, 0:D]
    TW1 = PL[:, 3 * T:3 * T + D]
    SCLB = WSM[0:64, 2 * D:3 * D]
    LAB = WSM[64:96, 2 * D:3 * D]
    LBB = WSM[64:96, D:2 * D]
    WQC = [WSM[:, 0:D], WSM[:, D:2 * D]]
    VT = PL[:, 3 * T:3 * T + 2 * D]

    p = Prog()
    import contextlib
    with contextlib.ExitStack() as stack:
        hL = [stack.enter_context(nc.semaphore(f"sL{i}")) for i in range(8)]
        hS = [stack.enter_context(nc.semaphore(f"sS{i}")) for i in range(2)]
        hPE = stack.enter_context(nc.semaphore("sPE"))
        hDV = stack.enter_context(nc.semaphore("sDV"))
        hAC = stack.enter_context(nc.semaphore("sAC"))
        block = stack.enter_context(nc.Block())
        sPE, sDV, sAC = S(hPE), S(hDV), S(hAC)
        p.autosems = {"vector": sDV, "scalar": sAC}
        LD = Ring(p, [S(h) for h in hL], "gpsimd")
        STR = Ring(p, [S(h) for h in hS], "sync")

        def waitt(eng, tok):
            if tok is not None:
                p.wait(eng, tok[0], tok[1])
        eps_col = SMALLS[:, SM_EPS:SM_EPS + 1]
        ones = CONSTS[:, CO_ONES:CO_ONES + 128]
        x_src = xt.rearrange("(c p) t -> p c t", p=128)

        tok_smalls = LD.issue(lambda e: e.dma_start(SMALLS[:, :], smalls))
        tok_consts = LD.issue(lambda e: e.dma_start(CONSTS[:, :], consts))
        waitt("tensor", tok_consts)
        waitt("scalar", tok_smalls)
        waitt("vector", tok_smalls)
        waitt("vector", tok_consts)
        waitt("tensor", tok_smalls)

        slot_free_pe = [None, None]
        dv_sq = None
        pe_last = None
        for c in range(NCH):
            sl = c % 2
            p.wait("gpsimd", sPE, slot_free_pe[sl])
            t_x = LD.issue(lambda e, c=c, sl=sl: e.dma_start(
                pl(sl), x_src[:, c, :]))
            waitt("vector", t_x)
            p.wait("vector", sPE, pe_last)
            dv_sq = p.op("vector", lambda e, sl=sl: e.tensor_tensor(
                pl(2), pl(sl), pl(sl), ALU.mult), inc=sDV)
            p.wait("tensor", sDV, dv_sq)
            for ts in range(NT):
                p.op("tensor", lambda e, sl=sl, ts=ts, c=c: e.matmul(
                    PS_A[:, ts * TS:(ts + 1) * TS], ones,
                    pl(sl)[:, ts * TS:(ts + 1) * TS],
                    start=(c == 0), stop=(c == NCH - 1), skip_group_check=True))
            for ts in range(NT):
                pe_last = p.op("tensor", lambda e, ts=ts, c=c: e.matmul(
                    PS_B[:, ts * TS:(ts + 1) * TS], ones,
                    pl(2)[:, ts * TS:(ts + 1) * TS],
                    start=(c == 0), stop=(c == NCH - 1), skip_group_check=True),
                    inc=sPE)
            slot_free_pe[sl] = pe_last

        def stats_tail(pe_stats):
            p.wait("scalar", sPE, pe_stats)
            ac_msq = p.op("scalar", lambda e: e.activation(
                SMF0, PS_A[:, :], AF.Square, scale=1.0 / D), inc=sAC)
            p.wait("vector", sAC, ac_msq)
            p.op("vector", lambda e: e.scalar_tensor_tensor(
                SMF1, PS_B[:, :], 1.0 / D, SMF0, ALU.mult, ALU.subtract))
            dv_var = p.op("vector", lambda e: e.tensor_scalar(
                pl(3), PS_A[:, :], 1.0 / D, None, ALU.mult), inc=sDV)
            p.wait("scalar", sDV, dv_var)
            p.sw("scalar")
            p.op("scalar", lambda e: e.activation(SMF0, SMF1, AF.Ln, bias=eps_col))
            p.sw("scalar")
            ac_rs = p.op("scalar", lambda e: e.activation(
                pl(4), SMF0, AF.Exp, scale=-0.5), inc=sAC)
            return dv_var, ac_rs

        ln0_stats_pe = pe_last
        ln0_psum_free, ln0_rs = stats_tail(pe_last)
        p.wait("vector", sAC, ln0_rs)
        p.wait("gpsimd", sPE, ln0_stats_pe)
        slot_free_dv = [None, None]
        ln0_out = None
        for c in range(NCH):
            sl = c % 2
            p.wait("gpsimd", sDV, slot_free_dv[sl])
            t_x = LD.issue(lambda e, c=c, sl=sl: e.dma_start(
                pl(sl), x_src[:, c, :]))
            waitt("vector", t_x)
            p.sw("vector")
            slot_free_dv[sl] = p.op("vector", lambda e, sl=sl: e.tensor_tensor(
                pl(2), pl(sl), pl(3), ALU.subtract), inc=sDV)
            p.sw("vector")
            ln0_out = p.op("vector", lambda e, c=c: e.tensor_tensor(
                G[0][:, c * T:(c + 1) * T], pl(2), pl(4), ALU.mult), inc=sDV)

        def emit_ln_gelu(src_big, dst_big, blk):
            pe_last = None
            dv_sq = None
            for c in range(NCH):
                p.wait("vector", sPE, pe_last)
                p.sw("vector")
                dv_sq = p.op("vector", lambda e, c=c: e.tensor_tensor(
                    pl(0), src_big[:, c * T:(c + 1) * T],
                    src_big[:, c * T:(c + 1) * T], ALU.mult), inc=sDV)
                p.wait("tensor", sDV, dv_sq)
                for ts in range(NT):
                    p.op("tensor", lambda e, c=c, ts=ts, src_big=src_big: e.matmul(
                        PS_A[:, ts * TS:(ts + 1) * TS], ones,
                        src_big[:, c * T + ts * TS: c * T + (ts + 1) * TS],
                        start=(c == 0), stop=(c == NCH - 1), skip_group_check=True))
                for ts in range(NT):
                    pe_last = p.op("tensor", lambda e, ts=ts, c=c: e.matmul(
                        PS_B[:, ts * TS:(ts + 1) * TS], ones,
                        pl(0)[:, ts * TS:(ts + 1) * TS],
                        start=(c == 0), stop=(c == NCH - 1), skip_group_check=True),
                        inc=sPE)
            dv_var, ac_rs = stats_tail(pe_last)
            p.wait("vector", sAC, ac_rs)
            prev_gelu = None
            for c in range(NCH):
                p.sw("vector")
                p.op("vector", lambda e, c=c: e.tensor_tensor(
                    pl(1), src_big[:, c * T:(c + 1) * T], pl(3), ALU.subtract))
                p.wait("vector", sAC, prev_gelu)
                p.sw("vector")
                tkv = p.op("vector", lambda e: e.tensor_tensor(
                    pl(2), pl(1), pl(4), ALU.mult), inc=sDV)
                p.wait("scalar", sDV, tkv)
                gcol = SMALLS[:, SM_GAMMA + blk * 8 + c: SM_GAMMA + blk * 8 + c + 1]
                bcol = SMALLS[:, SM_BETA + blk * 8 + c: SM_BETA + blk * 8 + c + 1]
                prev_gelu = p.op("scalar", lambda e, c=c, g_=gcol, b_=bcol: e.activation(
                    dst_big[:, c * T:(c + 1) * T], pl(2), AF.Gelu,
                    scale=g_, bias=b_), inc=sAC)
            return dv_var, prev_gelu

        wprep_dv_done = [None, None]
        wprep_pe_last = None
        wprep_last_dv1 = None

        def emit_wprep(j, par, psb_free=None):
            nonlocal wprep_pe_last, wprep_last_dv1
            p.wait("gpsimd", sPE, wprep_pe_last)
            t_sc = LD.issue(lambda e, j=j: e.dma_start(SCLB, sclT[j]))
            t_la = LD.issue(lambda e, j=j: e.dma_start(LAB, laT[j]))
            t_lb = LD.issue(lambda e, j=j: e.dma_start(LBB, lbT[j]))
            dv1 = wprep_last_dv1
            dv2 = psb_free
            waitt("tensor", t_sc)
            waitt("tensor", t_la)
            waitt("tensor", t_lb)
            for c in range(NCH):
                p.wait("gpsimd", sDV, dv1)
                t_q = LD.issue(lambda e, j=j, c=c: e.dma_start(
                    QBF, qT[j, c * 128:(c + 1) * 128, :]))
                p.wait("tensor", sDV, dv2)
                esel = CONSTS[0:64, CO_ESEL + c * 128: CO_ESEL + (c + 1) * 128]
                for half in range(2):
                    p.op("tensor", lambda e, half=half, esel=esel: e.matmul(
                        PS_B[:, half * TS:(half + 1) * TS], esel,
                        SCLB[:, half * TS:(half + 1) * TS], start=True, stop=True))
                pe_tok = None
                for half in range(2):
                    pe_tok = p.op("tensor", lambda e, c=c, half=half: e.matmul(
                        PS_B[:, 1024 + half * TS: 1024 + (half + 1) * TS],
                        LAB[:, c * 128:(c + 1) * 128],
                        LBB[:, half * TS:(half + 1) * TS], start=True, stop=True),
                        inc=sPE)
                wprep_pe_last = pe_tok
                p.wait("vector", sPE, pe_tok)
                waitt("vector", t_q)
                p.sw("vector")
                dv1 = p.op("vector", lambda e: e.scalar_tensor_tensor(
                    TW1, QBF, 7.5, PS_B[:, 0:1024], ALU.subtract, ALU.mult), inc=sDV)
                p.sw("vector")
                dv2 = p.op("vector", lambda e, par=par, c=c: e.scalar_tensor_tensor(
                    WEFF[par][:, c * D:(c + 1) * D], TW1, 1.0 / 7.5,
                    PS_B[:, 1024:2048], ALU.mult, ALU.add), inc=sDV)
            wprep_last_dv1 = dv1
            return dv2

        def emit_base_mm(par, in_big, evac):
            p.wait("tensor", sDV, wprep_dv_done[par])
            evac_tok = {}
            for m in range(NCH):
                for ts in range(NT):
                    if (m - 1, ts) in evac_tok:
                        p.wait("tensor", sDV, evac_tok[(m - 1, ts)])
                    pe_tok = None
                    for k in range(NCH):
                        pe_tok = p.op("tensor", lambda e, m=m, ts=ts, k=k, par=par,
                                      in_big=in_big: e.matmul(
                            PS_A[:, ts * TS:(ts + 1) * TS],
                            WEFF[par][:, k * D + m * 128: k * D + (m + 1) * 128],
                            in_big[:, k * T + ts * TS: k * T + (ts + 1) * TS],
                            start=(k == 0), stop=(k == NCH - 1)),
                            inc=(sPE if k == NCH - 1 else None))
                    evac_tok[(m, ts)] = evac(m, ts, pe_tok)
            return evac_tok

        h3_done = None
        for blk in range(3):
            j0, j1 = 2 * blk, 2 * blk + 1
            par0, par1 = j0 % 2, j1 % 2
            wprep_dv_done[par0] = emit_wprep(
                j0, par0, psb_free=(ln0_psum_free if blk == 0 else None))
            in_even = G[0] if blk == 0 else (G[3] if blk == 1 else G[1])
            ybuf = G[1] if blk < 2 else G[3]
            if blk == 0:
                p.wait("tensor", sDV, ln0_out)

            def evac_even(m, ts, pe_tok, j=j0, ybuf=ybuf):
                p.wait("vector", sPE, pe_tok)
                bcol = SMALLS[:, SM_BIAS + j * 8 + m: SM_BIAS + j * 8 + m + 1]
                return p.op("vector", lambda e, m=m, ts=ts, bcol=bcol, ybuf=ybuf:
                            e.tensor_scalar(
                                ybuf[:, m * T + ts * TS: m * T + (ts + 1) * TS],
                                PS_A[:, ts * TS:(ts + 1) * TS], bcol, None, ALU.add),
                            inc=sDV)
            emit_base_mm(par0, in_even, evac_even)

            wprep_dv_done[par1] = emit_wprep(j1, par1)
            ln_psum_free, gelu_out = emit_ln_gelu(ybuf, G[2], blk)

            hin = in_even
            hout = G[3] if blk == 0 else (G[1] if blk == 1 else G[3])

            def evac_odd(m, ts, pe_tok, j=j1, hin=hin, hout=hout):
                p.wait("vector", sPE, pe_tok)
                bcol = SMALLS[:, SM_BIAS + j * 8 + m: SM_BIAS + j * 8 + m + 1]
                return p.op("vector", lambda e, m=m, ts=ts, bcol=bcol, hin=hin, hout=hout:
                            e.scalar_tensor_tensor(
                                hout[:, m * T + ts * TS: m * T + (ts + 1) * TS],
                                PS_A[:, ts * TS:(ts + 1) * TS], bcol,
                                hin[:, m * T + ts * TS: m * T + (ts + 1) * TS],
                                ALU.add, ALU.add), inc=sDV)
            p.wait("tensor", sAC, gelu_out)
            p.wait("tensor", sDV, ln_psum_free)
            ev = emit_base_mm(par1, G[2], evac_odd)
            h3_done = ev[(NCH - 1, NT - 1)]

        p.wait("scalar", sDV, h3_done)
        silu_ac = None
        for c in range(NCH):
            silu_ac = p.op("scalar", lambda e, c=c: e.activation(
                G[1][:, c * T:(c + 1) * T], G[3][:, c * T:(c + 1) * T], AF.Silu),
                inc=sAC)

        p.wait("gpsimd", sDV, h3_done)
        wq_src = wqkvT.rearrange("(c p) o -> p c o", p=128)
        tok_wv = LD.issue(lambda e: e.dma_start(
            WVB.rearrange("p (c o) -> p c o", c=NCH),
            wq_src[:, :, 2 * D:3 * D]))
        tok_wp = LD.issue(lambda e: e.dma_start(
            WPB.rearrange("p (c o) -> p c o", c=NCH),
            wprojT.rearrange("(c p) o -> p c o", p=128)))

        wqc_tok = [None, None]
        qk_evac = {}
        qk_last = None
        for m in range(16):
            par = m % 2
            p.wait("gpsimd", sPE, wqc_tok[par])
            t_w = LD.issue(lambda e, m=m, par=par: e.dma_start(
                WQC[par].rearrange("p (c o) -> p c o", c=NCH),
                wq_src[:, :, m * 128:(m + 1) * 128]))
            waitt("tensor", t_w)
            dst = G[2] if m < 8 else G[3]
            mm_ = m % 8
            bias_off = SM_BQ if m < 8 else SM_BK
            last_pe_m = None
            for ts in range(NT):
                if (m - 2, ts) in qk_evac:
                    p.wait("tensor", sAC, qk_evac[(m - 2, ts)])
                ps = PS_A if (m % 2 == 0) else PS_B
                pe_tok = None
                for k in range(NCH):
                    pe_tok = p.op("tensor", lambda e, m=m, ts=ts, k=k, ps=ps: e.matmul(
                        ps[:, ts * TS:(ts + 1) * TS],
                        WQC[m % 2][:, k * 128:(k + 1) * 128],
                        G[0][:, k * T + ts * TS: k * T + (ts + 1) * TS],
                        start=(k == 0), stop=(k == NCH - 1)),
                        inc=(sPE if k == NCH - 1 else None))
                last_pe_m = pe_tok
                p.wait("scalar", sPE, pe_tok)
                bcol = SMALLS[:, bias_off + mm_: bias_off + mm_ + 1]
                qk_evac[(m, ts)] = p.op(
                    "scalar", lambda e, dst=dst, mm_=mm_, ts=ts, ps=ps, bcol=bcol:
                    e.activation(
                        dst[:, mm_ * T + ts * TS: mm_ * T + (ts + 1) * TS],
                        ps[:, ts * TS:(ts + 1) * TS], AF.Identity, bias=bcol), inc=sAC)
                qk_last = qk_evac[(m, ts)]
            wqc_tok[par] = last_pe_m

        waitt("tensor", tok_wp)
        p.wait("tensor", sAC, qk_last)

        a33 = CONSTS[0:33, CO_A33:CO_A33 + 128]
        ET, RT, ENT = pl(0), pl(1), pl(2)
        prev_av_evac = None
        prev_recipB = None
        prev_en = None
        prev_av_pe = None
        v_ev = [None, None]
        av_read = [None, None]
        waitt("tensor", tok_wv)
        ao_last = None
        for g in range(NG):
            sl = g % 2
            p.wait("tensor", sDV, v_ev[sl] if g >= 2 else None)
            if g >= 1:
                p.wait("tensor", sDV, v_ev[1 - sl])
            for half in range(2):
                pe_tok = None
                for k in range(NCH):
                    pe_tok = p.op("tensor", lambda e, g=g, half=half, k=k: e.matmul(
                        PS_B[:, half * TS:(half + 1) * TS],
                        G[0][:, k * T + g * 128: k * T + (g + 1) * 128],
                        WVB[:, k * D + half * TS: k * D + (half + 1) * TS],
                        start=(k == 0), stop=(k == NCH - 1)),
                        inc=(sPE if k == NCH - 1 else None))
                p.wait("vector", sPE, pe_tok)
                if g >= 2 and half == 0:
                    p.wait("vector", sPE, av_read[sl])
                v_ev[sl] = p.op("vector", lambda e, sl=sl, half=half: e.tensor_copy(
                    VT[:, sl * D + half * TS: sl * D + (half + 1) * TS],
                    PS_B[:, half * TS:(half + 1) * TS]), inc=sDV)
            p.wait("tensor", sAC, prev_av_evac)
            for ts in range(NT):
                p.op("tensor", lambda e, ts=ts: e.matmul(
                    PS_A[:, ts * TS:(ts + 1) * TS], a33,
                    CONSTS[0:33, CO_MV33: CO_MV33 + TS],
                    start=True, stop=False, skip_group_check=True))
            tk_qk = None
            for h in range(H):
                c = h // 2
                r0 = 64 * (h % 2)
                tk_qk = p.op("tensor", lambda e, g=g, h=h, c=c, r0=r0: e.matmul(
                    PS_A[:, h * 128:(h + 1) * 128],
                    G[3][r0:r0 + 64, c * T + g * 128: c * T + (g + 1) * 128],
                    G[2][r0:r0 + 64, c * T + g * 128: c * T + (g + 1) * 128],
                    start=False, stop=True, skip_group_check=True),
                    inc=(sPE if h == H - 1 else None))
            p.wait("scalar", sPE, tk_qk)
            p.wait("scalar", sDV, prev_en)
            tk_exp = None
            for half in range(2):
                tk_exp = p.op("scalar", lambda e, half=half: e.activation(
                    ET[:, half * 1024:(half + 1) * 1024],
                    PS_A[:, half * 1024:(half + 1) * 1024], AF.Exp, scale=0.125),
                    inc=(sAC if half == 1 else None))
            p.wait("tensor", sAC, tk_exp)
            tk_recip = None
            for half in range(2):
                p.wait("tensor", sDV, prev_recipB if half == 0 else tk_recip)
                tk_den = None
                for q4 in range(2):
                    tk_den = p.op("tensor", lambda e, half=half, q4=q4: e.matmul(
                        PS_B[:, 1024 + q4 * TS: 1024 + (q4 + 1) * TS], ones,
                        ET[:, half * 1024 + q4 * TS: half * 1024 + (q4 + 1) * TS],
                        start=True, stop=True),
                        inc=(sPE if q4 == 1 else None))
                p.wait("vector", sPE, tk_den)
                tk_recip = p.op("vector", lambda e, half=half: e.reciprocal(
                    RT[:, half * 1024:(half + 1) * 1024], PS_B[:, 1024:2048]), inc=sDV)
            prev_recipB = tk_recip
            p.wait("vector", sPE, prev_av_pe)
            p.sw("vector")
            prev_en = p.op("vector", lambda e: e.tensor_tensor(
                ENT[:, :], ET[:, :], RT[:, :], ALU.mult), inc=sDV)
            p.wait("tensor", sDV, prev_en)
            p.wait("tensor", sDV, v_ev[sl])
            tk_av = None
            for h in range(H):
                c = h // 2
                r0 = 64 * (h % 2)
                tk_av = p.op("tensor", lambda e, g=g, h=h, c=c, r0=r0, sl=sl: e.matmul(
                    PS_A[r0:r0 + 64, c * 128:(c + 1) * 128],
                    VT[:, sl * D + h * HD: sl * D + (h + 1) * HD],
                    ENT[:, h * 128:(h + 1) * 128],
                    start=True, stop=True, skip_group_check=True),
                    inc=(sPE if h == H - 1 else None))
            av_read[sl] = tk_av
            prev_av_pe = tk_av
            p.wait("scalar", sPE, tk_av)
            for c in range(NCH):
                bvcol = SMALLS[:, SM_BV + c: SM_BV + c + 1]
                prev_av_evac = p.op("scalar", lambda e, g=g, c=c, bvcol=bvcol: e.activation(
                    G[0][:, c * T + g * 128: c * T + (g + 1) * 128],
                    PS_A[:, c * 128:(c + 1) * 128], AF.Identity, bias=bvcol), inc=sAC)
            ao_last = prev_av_evac

        p.wait("tensor", sAC, ao_last)
        s_evac = {}
        for m in range(NCH):
            for ts in range(NT):
                if (m - 2, ts) in s_evac:
                    p.wait("tensor", sDV, s_evac[(m - 2, ts)])
                ps = PS_A if (m % 2 == 0) else PS_B
                psl = ps[:, ts * TS:(ts + 1) * TS]
                pe_tok = None
                for k in range(NCH):
                    pe_tok = p.op("tensor", lambda e, m=m, ts=ts, k=k, psl=psl: e.matmul(
                        psl,
                        WPB[:, k * D + m * 128: k * D + (m + 1) * 128],
                        G[0][:, k * T + ts * TS: k * T + (ts + 1) * TS],
                        start=(k == 0), stop=(k == NCH - 1)),
                        inc=(sPE if k == NCH - 1 else None))
                p.wait("vector", sPE, pe_tok)
                bpcol = SMALLS[:, SM_BPROJ + m: SM_BPROJ + m + 1]
                s_evac[(m, ts)] = p.op("vector", lambda e, m=m, ts=ts, psl=psl, bpcol=bpcol:
                                       e.tensor_scalar(
                                           G[2][:, m * T + ts * TS: m * T + (ts + 1) * TS],
                                           psl, bpcol, None, ALU.add), inc=sDV)

        p.wait("gpsimd", sAC, ao_last)
        xch_tok = [None, None]
        och_free = [None, None]
        st_total = None
        for c in range(NCH):
            par = c % 2
            p.wait("gpsimd", sDV, xch_tok[par])
            t_x = LD.issue(lambda e, c=c, par=par: e.dma_start(
                pl(par), x_src[:, c, :]))
            p.wait("vector", sDV, s_evac[(c, NT - 1)])
            p.wait("vector", sAC, silu_ac)
            waitt("vector", t_x)
            waitt("vector", och_free[par])
            p.sw("vector")
            p.op("vector", lambda e, c=c: e.tensor_tensor(
                pl(2), G[2][:, c * T:(c + 1) * T], G[1][:, c * T:(c + 1) * T], ALU.mult))
            p.sw("vector")
            xch_tok[par] = p.op("vector", lambda e, c=c, par=par: e.tensor_tensor(
                OCH[par], pl(par), pl(2), ALU.add), inc=sDV)
            p.wait("sync", sDV, xch_tok[par])
            tk_st = STR.issue(lambda e, c=c, par=par: e.dma_start(
                out[c * 128:(c + 1) * 128, :], OCH[par]))
            och_free[par] = tk_st
            st_total = tk_st

        for tname, ap in tap_out.items():
            idx = {"xn": 0, "g": 1, "q": 2, "k": 3, "ao": 0, "s": 2, "h3": 3,
                   "h1": 3, "y0": 1, "gelu": 2, "h2": 1}[tname]
            for c in range(NCH):
                tkv = p.op("vector", lambda e, c=c, idx=idx: e.tensor_copy(
                    OCH[0], G[idx][:, c * T:(c + 1) * T]), inc=sDV)
                p.wait("sync", sDV, tkv)
                tk = STR.issue(lambda e, c=c, ap=ap: e.dma_start(
                    ap[c * 128:(c + 1) * 128, :], OCH[0]))
                waitt("vector", tk)
                st_total = tk

        STR.drain("gpsimd")

        with nc.allow_low_precision(reason="bf16 compute sanctioned by problem tolerance"):
            @block.tensor
            def _(eng):
                for f in p.streams["tensor"]:
                    f(eng)

            @block.vector
            def _(eng):
                for f in p.streams["vector"]:
                    f(eng)

            @block.scalar
            def _(eng):
                for f in p.streams["scalar"]:
                    f(eng)

            @block.gpsimd
            def _(eng):
                for f in p.streams["gpsimd"]:
                    f(eng)

            @block.sync
            def _(eng):
                for f in p.streams["sync"]:
                    f(eng)

    return nc


def _host_pack(inputs):
    x = np.asarray(inputs["L0_attn_x"], dtype=np.float32)
    Wqkv = np.asarray(inputs["L0_attn_Wqkv"], dtype=np.float32)
    bqkv = np.asarray(inputs["L0_attn_bqkv"], dtype=np.float32)
    Wproj = np.asarray(inputs["L0_attn_Wproj"], dtype=np.float32)
    bproj = np.asarray(inputs["L0_attn_bproj"], dtype=np.float32)
    qw = np.asarray(inputs["L0_ffn_qweights"], dtype=np.int32)
    scl = np.asarray(inputs["L0_ffn_scales"], dtype=np.float32)
    bias = np.asarray(inputs["L0_ffn_biases"], dtype=np.float32)
    la = np.asarray(inputs["L0_ffn_lora_a"], dtype=np.float32)
    lb = np.asarray(inputs["L0_ffn_lora_b"], dtype=np.float32)
    gam = np.asarray(inputs["L0_ffn_ln_gamma"], dtype=np.float32)
    bet = np.asarray(inputs["L0_ffn_ln_beta"], dtype=np.float32)

    shared = {}
    shared["wqkvT"] = np.ascontiguousarray(Wqkv.T)
    shared["wprojT"] = np.ascontiguousarray(Wproj.T)
    shared["qT"] = np.ascontiguousarray(qw.transpose(0, 2, 1))
    shared["sclT"] = np.ascontiguousarray(scl.transpose(0, 2, 1))
    shared["laT"] = np.ascontiguousarray(la)
    shared["lbT"] = np.ascontiguousarray(lb.transpose(0, 2, 1))

    sm = np.zeros((128, SM_N), dtype=np.float32)
    sm[:, SM_BQ:SM_BQ + 8] = bqkv[0:1024].reshape(8, 128).T
    sm[:, SM_BK:SM_BK + 8] = bqkv[1024:2048].reshape(8, 128).T
    sm[:, SM_BPROJ:SM_BPROJ + 8] = bproj.reshape(8, 128).T
    for j in range(6):
        sm[:, SM_BIAS + j * 8: SM_BIAS + (j + 1) * 8] = bias[j].reshape(8, 128).T
    for blk in range(3):
        sm[:, SM_GAMMA + blk * 8: SM_GAMMA + (blk + 1) * 8] = gam[blk].reshape(8, 128).T
        sm[:, SM_BETA + blk * 8: SM_BETA + (blk + 1) * 8] = bet[blk].reshape(8, 128).T
    sm[:, SM_EPS] = 1e-5
    sm[:, SM_BV:SM_BV + 8] = bqkv[2048:3072].reshape(8, 128).T
    shared["smalls"] = sm

    co = np.zeros((128, CO_N), dtype=BF)
    co[:, CO_ONES:CO_ONES + 128] = 1.0
    A33 = np.zeros((33, 128), dtype=np.float32)
    P33 = np.zeros((33, 128), dtype=np.float32)
    r = np.arange(128)
    A33[0, :] = -128.0
    P33[0, :] = 128.0
    for b in range(16):
        inb = (r // 8 == b)
        A33[1 + 2 * b, :] = 128.0 * inb
        P33[1 + 2 * b, :] = 128.0 * inb
        A33[2 + 2 * b, :] = -128.0 * (inb & (r % 8 >= 4))
        P33[2 + 2 * b, :] = 128.0 * (inb & (r % 8 < 4))
    co[0:33, CO_A33:CO_A33 + 128] = A33.astype(BF)
    co[0:33, CO_MV33:CO_MV33 + 512] = np.tile(P33, (1, 4)).astype(BF)
    for c in range(NCH):
        E = np.zeros((64, 128), dtype=np.float32)
        d = np.arange(128)
        E[(8 * c + d // 16), d] = 1.0
        co[0:64, CO_ESEL + c * 128: CO_ESEL + (c + 1) * 128] = E.astype(BF)
    shared["consts"] = co

    per_core = []
    for core in range(NCORES):
        xs = x[:, core * BLOC:(core + 1) * BLOC, :]
        xtc = np.ascontiguousarray(xs.transpose(2, 1, 0).reshape(D, T))
        per_core.append({"xt": xtc})
    return shared, per_core


def kernel(**inputs):
    shared, per_core = _host_pack(inputs)
    nc = build_nc()
    in_maps = [{**shared, **pc} for pc in per_core]
    res = run_bass_kernel_spmd(nc, in_maps, core_ids=list(range(NCORES)))
    outs = []
    for core in range(NCORES):
        o = np.asarray(res.results[core]["o"])
        outs.append(o.reshape(D, BLOC, SEQN).transpose(2, 1, 0))
    return np.concatenate(outs, axis=1).astype(np.float32)


# version 06
# speedup vs baseline: 1.1224x; 1.1223x over previous
import numpy as np
import ml_dtypes
import concourse.bass as bass
import concourse.mybir as mybir
from concourse.bass_utils import run_bass_kernel_spmd

F32 = mybir.dt.float32
BF16 = mybir.dt.bfloat16
I32 = mybir.dt.int32
AF = mybir.ActivationFunctionType
ALU = mybir.AluOpType
BF = ml_dtypes.bfloat16

NCORES = 8
D = 1024
NCH = 8
SEQN = 8
BLOC = 256
T = 2048
NT = 4
TS = 512
NG = 16
H = 16
HD = 64

SM_BQ = 0
SM_BK = 8
SM_BPROJ = 16
SM_BIAS = 24
SM_GAMMA = 72
SM_BETA = 96
SM_EPS = 120
SM_BV = 121
SM_N = 136

CO_ONES = 0
CO_A33 = 128
CO_MV33 = 256
CO_ESEL = 768
CO_BV = 1792
CO_N = 1800


class S:
    def __init__(self, h):
        self.h = h
        self.n = 0


class Prog:
    ENGINES = ("tensor", "vector", "scalar", "gpsimd", "sync")

    def __init__(self):
        self.streams = {e: [] for e in self.ENGINES}
        self.autosems = {}

    def wait(self, eng, sem, val):
        if val is None or val <= 0:
            return
        self.streams[eng].append(lambda e, s=sem, v=val: e.wait_ge(s.h, v))

    def op(self, eng, fn, inc=None, k=1):
        if inc is None:
            inc = self.autosems.get(eng)
            if inc is None:
                self.streams[eng].append(fn)
                return None
        inc.n += k
        tok = inc.n
        self.streams[eng].append(lambda e, f=fn, s=inc, kk=k: f(e).then_inc(s.h, kk))
        return tok

    def sw(self, eng):
        sem = self.autosems[eng]
        self.wait(eng, sem, sem.n)


class Ring:
    def __init__(self, p, sems, queue):
        self.p = p
        self.sems = sems
        self.queue = queue
        self.i = 0

    def issue(self, fn):
        r = self.i % len(self.sems)
        n = self.i // len(self.sems)
        self.i += 1
        sem = self.sems[r]
        if n > 0:
            self.p.wait(self.queue, sem, 16 * n)
        self.p.op(self.queue, fn, inc=sem, k=16)
        return (sem, 16 * (n + 1))

    def drain(self, eng):
        for r, sem in enumerate(self.sems):
            tot = (self.i - 1 - r) // len(self.sems) + 1 if r < self.i else 0
            if tot > 0:
                self.p.wait(eng, sem, 16 * tot)


def build_nc(taps=()):
    nc = bass.Bass("TRN2", target_bir_lowering=False)

    xt = nc.dram_tensor("xt", [D, T], F32, kind="ExternalInput").ap()
    wqkvT = nc.dram_tensor("wqkvT", [D, 3 * D], F32, kind="ExternalInput").ap()
    wprojT = nc.dram_tensor("wprojT", [D, D], F32, kind="ExternalInput").ap()
    qT = nc.dram_tensor("qT", [6, D, D], I32, kind="ExternalInput").ap()
    sclT = nc.dram_tensor("sclT", [6, 64, D], F32, kind="ExternalInput").ap()
    laT = nc.dram_tensor("laT", [6, 32, D], F32, kind="ExternalInput").ap()
    lbT = nc.dram_tensor("lbT", [6, 32, D], F32, kind="ExternalInput").ap()
    smalls = nc.dram_tensor("smalls", [128, SM_N], F32, kind="ExternalInput").ap()
    consts = nc.dram_tensor("consts", [128, CO_N], BF16, kind="ExternalInput").ap()
    out = nc.dram_tensor("o", [D, T], F32, kind="ExternalOutput").ap()
    tap_out = {}
    for tname in taps:
        tap_out[tname] = nc.dram_tensor("tap_" + tname, [D, T], F32,
                                        kind="ExternalOutput").ap()

    G = [nc.alloc_sbuf_tensor(f"big{i}", [128, NCH * T], BF16) for i in range(4)]
    WPOOL = nc.alloc_sbuf_tensor("wpool", [128, 2 * NCH * D], BF16)
    PL = nc.alloc_sbuf_tensor("pl", [128, 5 * T], BF16)
    OCHP = nc.alloc_sbuf_tensor("ochp", [128, 2 * T], F32)
    WSM = nc.alloc_sbuf_tensor("wsm", [128, 3 * D], BF16)
    SMALLS = nc.alloc_sbuf_tensor("smalls_sb", [128, SM_N], F32)
    CONSTS = nc.alloc_sbuf_tensor("consts_sb", [128, CO_N], BF16)

    PS_A = nc.alloc_psum_tensor("psA", [128, T], F32)
    PS_B = nc.alloc_psum_tensor("psB", [128, T], F32)

    def pl(i):
        return PL[:, i * T:(i + 1) * T]
    WEFF = [WPOOL[:, 0:NCH * D], WPOOL[:, NCH * D:2 * NCH * D]]
    WVB = WPOOL[:, 0:NCH * D]
    WPB = WPOOL[:, NCH * D:2 * NCH * D]
    SMF0 = OCHP[:, 0:T]
    SMF1 = OCHP[:, T:2 * T]
    OCH = [OCHP[:, 0:T], OCHP[:, T:2 * T]]
    QBF = WSM[:, 0:D]
    TW1 = PL[:, 3 * T:3 * T + D]
    SCLB = WSM[0:64, 2 * D:3 * D]
    LAB = WSM[64:96, 2 * D:3 * D]
    LBB = WSM[64:96, D:2 * D]
    WQC = [WSM[:, 0:D], WSM[:, D:2 * D]]
    VT = PL[:, 3 * T:3 * T + 2 * D]

    p = Prog()
    import contextlib
    with contextlib.ExitStack() as stack:
        hL = [stack.enter_context(nc.semaphore(f"sL{i}")) for i in range(8)]
        hS = [stack.enter_context(nc.semaphore(f"sS{i}")) for i in range(2)]
        hPE = stack.enter_context(nc.semaphore("sPE"))
        hDV = stack.enter_context(nc.semaphore("sDV"))
        hAC = stack.enter_context(nc.semaphore("sAC"))
        block = stack.enter_context(nc.Block())
        sPE, sDV, sAC = S(hPE), S(hDV), S(hAC)
        p.autosems = {"vector": sDV, "scalar": sAC}
        LD = Ring(p, [S(h) for h in hL], "gpsimd")
        STR = Ring(p, [S(h) for h in hS], "sync")

        def waitt(eng, tok):
            if tok is not None:
                p.wait(eng, tok[0], tok[1])
        eps_col = SMALLS[:, SM_EPS:SM_EPS + 1]
        ones = CONSTS[:, CO_ONES:CO_ONES + 128]
        x_src = xt.rearrange("(c p) t -> p c t", p=128)

        tok_smalls = LD.issue(lambda e: e.dma_start(SMALLS[:, :], smalls))
        tok_consts = LD.issue(lambda e: e.dma_start(CONSTS[:, :], consts))
        waitt("tensor", tok_consts)
        waitt("scalar", tok_smalls)
        waitt("vector", tok_smalls)
        waitt("vector", tok_consts)
        waitt("tensor", tok_smalls)

        slot_free_pe = [None, None]
        dv_sq = None
        pe_last = None
        for c in range(NCH):
            sl = c % 2
            p.wait("gpsimd", sPE, slot_free_pe[sl])
            t_x = LD.issue(lambda e, c=c, sl=sl: e.dma_start(
                pl(sl), x_src[:, c, :]))
            waitt("vector", t_x)
            p.wait("vector", sPE, pe_last)
            dv_sq = p.op("vector", lambda e, sl=sl: e.tensor_tensor(
                pl(2), pl(sl), pl(sl), ALU.mult), inc=sDV)
            p.wait("tensor", sDV, dv_sq)
            for ts in range(NT):
                p.op("tensor", lambda e, sl=sl, ts=ts, c=c: e.matmul(
                    PS_A[:, ts * TS:(ts + 1) * TS], ones,
                    pl(sl)[:, ts * TS:(ts + 1) * TS],
                    start=(c == 0), stop=(c == NCH - 1), skip_group_check=True))
            for ts in range(NT):
                pe_last = p.op("tensor", lambda e, ts=ts, c=c: e.matmul(
                    PS_B[:, ts * TS:(ts + 1) * TS], ones,
                    pl(2)[:, ts * TS:(ts + 1) * TS],
                    start=(c == 0), stop=(c == NCH - 1), skip_group_check=True),
                    inc=sPE)
            slot_free_pe[sl] = pe_last

        def stats_tail(pe_stats):
            p.wait("scalar", sPE, pe_stats)
            ac_msq = p.op("scalar", lambda e: e.activation(
                SMF0, PS_A[:, :], AF.Square, scale=1.0 / D), inc=sAC)
            p.wait("vector", sAC, ac_msq)
            p.op("vector", lambda e: e.scalar_tensor_tensor(
                SMF1, PS_B[:, :], 1.0 / D, SMF0, ALU.mult, ALU.subtract))
            dv_var = p.op("vector", lambda e: e.tensor_scalar(
                pl(3), PS_A[:, :], 1.0 / D, None, ALU.mult), inc=sDV)
            p.wait("scalar", sDV, dv_var)
            p.sw("scalar")
            p.op("scalar", lambda e: e.activation(SMF0, SMF1, AF.Ln, bias=eps_col))
            p.sw("scalar")
            ac_rs = p.op("scalar", lambda e: e.activation(
                pl(4), SMF0, AF.Exp, scale=-0.5), inc=sAC)
            return dv_var, ac_rs

        ln0_stats_pe = pe_last
        ln0_psum_free, ln0_rs = stats_tail(pe_last)
        p.wait("vector", sAC, ln0_rs)
        p.wait("gpsimd", sPE, ln0_stats_pe)
        slot_free_dv = [None, None]
        ln0_out = None
        for c in range(NCH):
            sl = c % 2
            p.wait("gpsimd", sDV, slot_free_dv[sl])
            t_x = LD.issue(lambda e, c=c, sl=sl: e.dma_start(
                pl(sl), x_src[:, c, :]))
            waitt("vector", t_x)
            p.sw("vector")
            slot_free_dv[sl] = p.op("vector", lambda e, sl=sl: e.tensor_tensor(
                pl(2), pl(sl), pl(3), ALU.subtract), inc=sDV)
            p.sw("vector")
            ln0_out = p.op("vector", lambda e, c=c: e.tensor_tensor(
                G[0][:, c * T:(c + 1) * T], pl(2), pl(4), ALU.mult), inc=sDV)

        def emit_ln_gelu(src_big, dst_big, blk):
            pe_last = None
            dv_sq = None
            for c in range(NCH):
                p.wait("vector", sPE, pe_last)
                p.sw("vector")
                dv_sq = p.op("vector", lambda e, c=c: e.tensor_tensor(
                    pl(0), src_big[:, c * T:(c + 1) * T],
                    src_big[:, c * T:(c + 1) * T], ALU.mult), inc=sDV)
                p.wait("tensor", sDV, dv_sq)
                for ts in range(NT):
                    p.op("tensor", lambda e, c=c, ts=ts, src_big=src_big: e.matmul(
                        PS_A[:, ts * TS:(ts + 1) * TS], ones,
                        src_big[:, c * T + ts * TS: c * T + (ts + 1) * TS],
                        start=(c == 0), stop=(c == NCH - 1), skip_group_check=True))
                for ts in range(NT):
                    pe_last = p.op("tensor", lambda e, ts=ts, c=c: e.matmul(
                        PS_B[:, ts * TS:(ts + 1) * TS], ones,
                        pl(0)[:, ts * TS:(ts + 1) * TS],
                        start=(c == 0), stop=(c == NCH - 1), skip_group_check=True),
                        inc=sPE)
            dv_var, ac_rs = stats_tail(pe_last)
            p.wait("vector", sAC, ac_rs)
            prev_gelu = None
            for c in range(NCH):
                p.sw("vector")
                p.op("vector", lambda e, c=c: e.tensor_tensor(
                    pl(1), src_big[:, c * T:(c + 1) * T], pl(3), ALU.subtract))
                p.wait("vector", sAC, prev_gelu)
                p.sw("vector")
                tkv = p.op("vector", lambda e: e.tensor_tensor(
                    pl(2), pl(1), pl(4), ALU.mult), inc=sDV)
                p.wait("scalar", sDV, tkv)
                gcol = SMALLS[:, SM_GAMMA + blk * 8 + c: SM_GAMMA + blk * 8 + c + 1]
                bcol = SMALLS[:, SM_BETA + blk * 8 + c: SM_BETA + blk * 8 + c + 1]
                prev_gelu = p.op("scalar", lambda e, c=c, g_=gcol, b_=bcol: e.activation(
                    dst_big[:, c * T:(c + 1) * T], pl(2), AF.Gelu,
                    scale=g_, bias=b_), inc=sAC)
            return dv_var, prev_gelu

        wprep_dv_done = [None, None]
        wprep_pe_last = None
        wprep_last_dv1 = None

        def emit_wprep(j, par, psb_free=None):
            nonlocal wprep_pe_last, wprep_last_dv1
            p.wait("gpsimd", sPE, wprep_pe_last)
            t_sc = LD.issue(lambda e, j=j: e.dma_start(SCLB, sclT[j]))
            t_la = LD.issue(lambda e, j=j: e.dma_start(LAB, laT[j]))
            t_lb = LD.issue(lambda e, j=j: e.dma_start(LBB, lbT[j]))
            dv1 = wprep_last_dv1
            dv2 = psb_free
            waitt("tensor", t_sc)
            waitt("tensor", t_la)
            waitt("tensor", t_lb)
            for c in range(NCH):
                p.wait("gpsimd", sDV, dv1)
                t_q = LD.issue(lambda e, j=j, c=c: e.dma_start(
                    QBF, qT[j, c * 128:(c + 1) * 128, :]))
                p.wait("tensor", sDV, dv2)
                esel = CONSTS[0:64, CO_ESEL + c * 128: CO_ESEL + (c + 1) * 128]
                for half in range(2):
                    p.op("tensor", lambda e, half=half, esel=esel: e.matmul(
                        PS_B[:, half * TS:(half + 1) * TS], esel,
                        SCLB[:, half * TS:(half + 1) * TS], start=True, stop=True))
                pe_tok = None
                for half in range(2):
                    pe_tok = p.op("tensor", lambda e, c=c, half=half: e.matmul(
                        PS_B[:, 1024 + half * TS: 1024 + (half + 1) * TS],
                        LAB[:, c * 128:(c + 1) * 128],
                        LBB[:, half * TS:(half + 1) * TS], start=True, stop=True),
                        inc=sPE)
                wprep_pe_last = pe_tok
                p.wait("vector", sPE, pe_tok)
                waitt("vector", t_q)
                p.sw("vector")
                dv1 = p.op("vector", lambda e: e.scalar_tensor_tensor(
                    TW1, QBF, 7.5, PS_B[:, 0:1024], ALU.subtract, ALU.mult), inc=sDV)
                p.sw("vector")
                dv2 = p.op("vector", lambda e, par=par, c=c: e.scalar_tensor_tensor(
                    WEFF[par][:, c * D:(c + 1) * D], TW1, 1.0 / 7.5,
                    PS_B[:, 1024:2048], ALU.mult, ALU.add), inc=sDV)
            wprep_last_dv1 = dv1
            return dv2

        def emit_base_mm(par, in_big, evac):
            p.wait("tensor", sDV, wprep_dv_done[par])
            evac_tok = {}
            for m in range(NCH):
                for ts in range(NT):
                    if (m - 1, ts) in evac_tok:
                        p.wait("tensor", sDV, evac_tok[(m - 1, ts)])
                    pe_tok = None
                    for k in range(NCH):
                        pe_tok = p.op("tensor", lambda e, m=m, ts=ts, k=k, par=par,
                                      in_big=in_big: e.matmul(
                            PS_A[:, ts * TS:(ts + 1) * TS],
                            WEFF[par][:, k * D + m * 128: k * D + (m + 1) * 128],
                            in_big[:, k * T + ts * TS: k * T + (ts + 1) * TS],
                            start=(k == 0), stop=(k == NCH - 1)),
                            inc=(sPE if k == NCH - 1 else None))
                    evac_tok[(m, ts)] = evac(m, ts, pe_tok)
            return evac_tok

        h3_done = None
        for blk in range(3):
            j0, j1 = 2 * blk, 2 * blk + 1
            par0, par1 = j0 % 2, j1 % 2
            wprep_dv_done[par0] = emit_wprep(
                j0, par0, psb_free=(ln0_psum_free if blk == 0 else None))
            in_even = G[0] if blk == 0 else (G[3] if blk == 1 else G[1])
            ybuf = G[1] if blk < 2 else G[3]
            if blk == 0:
                p.wait("tensor", sDV, ln0_out)

            def evac_even(m, ts, pe_tok, j=j0, ybuf=ybuf):
                p.wait("vector", sPE, pe_tok)
                bcol = SMALLS[:, SM_BIAS + j * 8 + m: SM_BIAS + j * 8 + m + 1]
                return p.op("vector", lambda e, m=m, ts=ts, bcol=bcol, ybuf=ybuf:
                            e.tensor_scalar(
                                ybuf[:, m * T + ts * TS: m * T + (ts + 1) * TS],
                                PS_A[:, ts * TS:(ts + 1) * TS], bcol, None, ALU.add),
                            inc=sDV)
            emit_base_mm(par0, in_even, evac_even)

            wprep_dv_done[par1] = emit_wprep(j1, par1)
            ln_psum_free, gelu_out = emit_ln_gelu(ybuf, G[2], blk)

            hin = in_even
            hout = G[3] if blk == 0 else (G[1] if blk == 1 else G[3])

            def evac_odd(m, ts, pe_tok, j=j1, hin=hin, hout=hout):
                p.wait("vector", sPE, pe_tok)
                bcol = SMALLS[:, SM_BIAS + j * 8 + m: SM_BIAS + j * 8 + m + 1]
                return p.op("vector", lambda e, m=m, ts=ts, bcol=bcol, hin=hin, hout=hout:
                            e.scalar_tensor_tensor(
                                hout[:, m * T + ts * TS: m * T + (ts + 1) * TS],
                                PS_A[:, ts * TS:(ts + 1) * TS], bcol,
                                hin[:, m * T + ts * TS: m * T + (ts + 1) * TS],
                                ALU.add, ALU.add), inc=sDV)
            p.wait("tensor", sAC, gelu_out)
            p.wait("tensor", sDV, ln_psum_free)
            ev = emit_base_mm(par1, G[2], evac_odd)
            h3_done = ev[(NCH - 1, NT - 1)]

        p.wait("scalar", sDV, h3_done)
        silu_ac = None
        for c in range(NCH):
            silu_ac = p.op("scalar", lambda e, c=c: e.activation(
                G[1][:, c * T:(c + 1) * T], G[3][:, c * T:(c + 1) * T], AF.Silu),
                inc=sAC)

        p.wait("gpsimd", sDV, h3_done)
        wq_src = wqkvT.rearrange("(c p) o -> p c o", p=128)
        tok_wv = LD.issue(lambda e: e.dma_start(
            WVB.rearrange("p (c o) -> p c o", c=NCH),
            wq_src[:, :, 2 * D:3 * D]))
        tok_wp = LD.issue(lambda e: e.dma_start(
            WPB.rearrange("p (c o) -> p c o", c=NCH),
            wprojT.rearrange("(c p) o -> p c o", p=128)))

        wqc_tok = [None, None]
        qk_evac = {}
        qk_last = None
        for m in range(16):
            par = m % 2
            p.wait("gpsimd", sPE, wqc_tok[par])
            t_w = LD.issue(lambda e, m=m, par=par: e.dma_start(
                WQC[par].rearrange("p (c o) -> p c o", c=NCH),
                wq_src[:, :, m * 128:(m + 1) * 128]))
            waitt("tensor", t_w)
            dst = G[2] if m < 8 else G[3]
            mm_ = m % 8
            bias_off = SM_BQ if m < 8 else SM_BK
            last_pe_m = None
            for ts in range(NT):
                if (m - 2, ts) in qk_evac:
                    p.wait("tensor", sAC, qk_evac[(m - 2, ts)])
                ps = PS_A if (m % 2 == 0) else PS_B
                pe_tok = None
                for k in range(NCH):
                    pe_tok = p.op("tensor", lambda e, m=m, ts=ts, k=k, ps=ps: e.matmul(
                        ps[:, ts * TS:(ts + 1) * TS],
                        WQC[m % 2][:, k * 128:(k + 1) * 128],
                        G[0][:, k * T + ts * TS: k * T + (ts + 1) * TS],
                        start=(k == 0), stop=(k == NCH - 1)),
                        inc=(sPE if k == NCH - 1 else None))
                last_pe_m = pe_tok
                p.wait("scalar", sPE, pe_tok)
                bcol = SMALLS[:, bias_off + mm_: bias_off + mm_ + 1]
                qk_evac[(m, ts)] = p.op(
                    "scalar", lambda e, dst=dst, mm_=mm_, ts=ts, ps=ps, bcol=bcol:
                    e.activation(
                        dst[:, mm_ * T + ts * TS: mm_ * T + (ts + 1) * TS],
                        ps[:, ts * TS:(ts + 1) * TS], AF.Identity, bias=bcol), inc=sAC)
                qk_last = qk_evac[(m, ts)]
            wqc_tok[par] = last_pe_m

        waitt("tensor", tok_wp)
        p.wait("tensor", sAC, qk_last)

        a33 = CONSTS[0:33, CO_A33:CO_A33 + 128]
        ET, RT, ENT = pl(0), pl(1), pl(2)
        prev_av_evac = None
        prev_recipB = None
        prev_en = None
        prev_av_pe = None
        v_ev = [None, None]
        av_read = [None, None]
        waitt("tensor", tok_wv)
        ao_last = None
        for g in range(NG):
            sl = g % 2
            p.wait("tensor", sDV, v_ev[sl] if g >= 2 else None)
            if g >= 1:
                p.wait("tensor", sDV, v_ev[1 - sl])
            for half in range(2):
                pe_tok = None
                for k in range(NCH):
                    pe_tok = p.op("tensor", lambda e, g=g, half=half, k=k: e.matmul(
                        PS_B[:, half * TS:(half + 1) * TS],
                        G[0][:, k * T + g * 128: k * T + (g + 1) * 128],
                        WVB[:, k * D + half * TS: k * D + (half + 1) * TS],
                        start=(k == 0), stop=(k == NCH - 1)),
                        inc=(sPE if k == NCH - 1 else None))
                p.wait("vector", sPE, pe_tok)
                if g >= 2 and half == 0:
                    p.wait("vector", sPE, av_read[sl])
                v_ev[sl] = p.op("vector", lambda e, sl=sl, half=half: e.tensor_copy(
                    VT[:, sl * D + half * TS: sl * D + (half + 1) * TS],
                    PS_B[:, half * TS:(half + 1) * TS]), inc=sDV)
            p.wait("tensor", sAC, prev_av_evac)
            for ts in range(NT):
                p.op("tensor", lambda e, ts=ts: e.matmul(
                    PS_A[:, ts * TS:(ts + 1) * TS], a33,
                    CONSTS[0:33, CO_MV33: CO_MV33 + TS],
                    start=True, stop=False, skip_group_check=True))
            tk_qk = None
            for h in range(H):
                c = h // 2
                r0 = 64 * (h % 2)
                tk_qk = p.op("tensor", lambda e, g=g, h=h, c=c, r0=r0: e.matmul(
                    PS_A[:, h * 128:(h + 1) * 128],
                    G[3][r0:r0 + 64, c * T + g * 128: c * T + (g + 1) * 128],
                    G[2][r0:r0 + 64, c * T + g * 128: c * T + (g + 1) * 128],
                    start=False, stop=True, skip_group_check=True),
                    inc=(sPE if h == H - 1 else None))
            p.wait("scalar", sPE, tk_qk)
            p.wait("scalar", sDV, prev_en)
            tk_exp = None
            for half in range(2):
                tk_exp = p.op("scalar", lambda e, half=half: e.activation(
                    ET[:, half * 1024:(half + 1) * 1024],
                    PS_A[:, half * 1024:(half + 1) * 1024], AF.Exp, scale=0.125),
                    inc=(sAC if half == 1 else None))
            p.wait("tensor", sAC, tk_exp)
            tk_rt = None
            for half in range(2):
                p.wait("tensor", sAC, prev_recipB if half == 0 else tk_ln)
                tk_den = None
                for q4 in range(2):
                    tk_den = p.op("tensor", lambda e, half=half, q4=q4: e.matmul(
                        PS_B[:, 1024 + q4 * TS: 1024 + (q4 + 1) * TS], ones,
                        ET[:, half * 1024 + q4 * TS: half * 1024 + (q4 + 1) * TS],
                        start=True, stop=True),
                        inc=(sPE if q4 == 1 else None))
                p.wait("scalar", sPE, tk_den)
                p.sw("scalar")
                tk_ln = p.op("scalar", lambda e, half=half: e.activation(
                    SMF0[:, half * 1024:(half + 1) * 1024], PS_B[:, 1024:2048],
                    AF.Ln), inc=sAC)
                p.sw("scalar")
                tk_rt = p.op("scalar", lambda e, half=half: e.activation(
                    RT[:, half * 1024:(half + 1) * 1024],
                    SMF0[:, half * 1024:(half + 1) * 1024],
                    AF.Exp, scale=-1.0), inc=sAC)
            prev_recipB = tk_ln
            p.wait("vector", sPE, prev_av_pe)
            p.wait("vector", sAC, tk_rt)
            p.sw("vector")
            prev_en = p.op("vector", lambda e: e.tensor_tensor(
                ENT[:, :], ET[:, :], RT[:, :], ALU.mult), inc=sDV)
            p.wait("tensor", sDV, prev_en)
            p.wait("tensor", sDV, v_ev[sl])
            tk_av = None
            for h in range(H):
                c = h // 2
                r0 = 64 * (h % 2)
                tk_av = p.op("tensor", lambda e, g=g, h=h, c=c, r0=r0, sl=sl: e.matmul(
                    PS_A[r0:r0 + 64, c * 128:(c + 1) * 128],
                    VT[:, sl * D + h * HD: sl * D + (h + 1) * HD],
                    ENT[:, h * 128:(h + 1) * 128],
                    start=True, stop=True, skip_group_check=True),
                    inc=(sPE if h == H - 1 else None))
            av_read[sl] = tk_av
            prev_av_pe = tk_av
            p.wait("scalar", sPE, tk_av)
            for c in range(NCH):
                bvcol = SMALLS[:, SM_BV + c: SM_BV + c + 1]
                prev_av_evac = p.op("scalar", lambda e, g=g, c=c, bvcol=bvcol: e.activation(
                    G[0][:, c * T + g * 128: c * T + (g + 1) * 128],
                    PS_A[:, c * 128:(c + 1) * 128], AF.Identity, bias=bvcol), inc=sAC)
            ao_last = prev_av_evac

        p.wait("tensor", sAC, ao_last)
        s_evac = {}
        for m in range(NCH):
            for ts in range(NT):
                if (m - 2, ts) in s_evac:
                    p.wait("tensor", sDV, s_evac[(m - 2, ts)])
                ps = PS_A if (m % 2 == 0) else PS_B
                psl = ps[:, ts * TS:(ts + 1) * TS]
                pe_tok = None
                for k in range(NCH):
                    pe_tok = p.op("tensor", lambda e, m=m, ts=ts, k=k, psl=psl: e.matmul(
                        psl,
                        WPB[:, k * D + m * 128: k * D + (m + 1) * 128],
                        G[0][:, k * T + ts * TS: k * T + (ts + 1) * TS],
                        start=(k == 0), stop=(k == NCH - 1)),
                        inc=(sPE if k == NCH - 1 else None))
                p.wait("vector", sPE, pe_tok)
                bpcol = SMALLS[:, SM_BPROJ + m: SM_BPROJ + m + 1]
                s_evac[(m, ts)] = p.op("vector", lambda e, m=m, ts=ts, psl=psl, bpcol=bpcol:
                                       e.tensor_scalar(
                                           G[2][:, m * T + ts * TS: m * T + (ts + 1) * TS],
                                           psl, bpcol, None, ALU.add), inc=sDV)

        p.wait("gpsimd", sAC, ao_last)
        xch_tok = [None, None]
        och_free = [None, None]
        st_total = None
        for c in range(NCH):
            par = c % 2
            p.wait("gpsimd", sDV, xch_tok[par])
            t_x = LD.issue(lambda e, c=c, par=par: e.dma_start(
                pl(par), x_src[:, c, :]))
            p.wait("vector", sDV, s_evac[(c, NT - 1)])
            p.wait("vector", sAC, silu_ac)
            waitt("vector", t_x)
            waitt("vector", och_free[par])
            p.sw("vector")
            p.op("vector", lambda e, c=c: e.tensor_tensor(
                pl(2), G[2][:, c * T:(c + 1) * T], G[1][:, c * T:(c + 1) * T], ALU.mult))
            p.sw("vector")
            xch_tok[par] = p.op("vector", lambda e, c=c, par=par: e.tensor_tensor(
                OCH[par], pl(par), pl(2), ALU.add), inc=sDV)
            p.wait("sync", sDV, xch_tok[par])
            tk_st = STR.issue(lambda e, c=c, par=par: e.dma_start(
                out[c * 128:(c + 1) * 128, :], OCH[par]))
            och_free[par] = tk_st
            st_total = tk_st

        for tname, ap in tap_out.items():
            idx = {"xn": 0, "g": 1, "q": 2, "k": 3, "ao": 0, "s": 2, "h3": 3,
                   "h1": 3, "y0": 1, "gelu": 2, "h2": 1}[tname]
            for c in range(NCH):
                tkv = p.op("vector", lambda e, c=c, idx=idx: e.tensor_copy(
                    OCH[0], G[idx][:, c * T:(c + 1) * T]), inc=sDV)
                p.wait("sync", sDV, tkv)
                tk = STR.issue(lambda e, c=c, ap=ap: e.dma_start(
                    ap[c * 128:(c + 1) * 128, :], OCH[0]))
                waitt("vector", tk)
                st_total = tk

        STR.drain("gpsimd")

        with nc.allow_low_precision(reason="bf16 compute sanctioned by problem tolerance"):
            @block.tensor
            def _(eng):
                for f in p.streams["tensor"]:
                    f(eng)

            @block.vector
            def _(eng):
                for f in p.streams["vector"]:
                    f(eng)

            @block.scalar
            def _(eng):
                for f in p.streams["scalar"]:
                    f(eng)

            @block.gpsimd
            def _(eng):
                for f in p.streams["gpsimd"]:
                    f(eng)

            @block.sync
            def _(eng):
                for f in p.streams["sync"]:
                    f(eng)

    return nc


def _host_pack(inputs):
    x = np.asarray(inputs["L0_attn_x"], dtype=np.float32)
    Wqkv = np.asarray(inputs["L0_attn_Wqkv"], dtype=np.float32)
    bqkv = np.asarray(inputs["L0_attn_bqkv"], dtype=np.float32)
    Wproj = np.asarray(inputs["L0_attn_Wproj"], dtype=np.float32)
    bproj = np.asarray(inputs["L0_attn_bproj"], dtype=np.float32)
    qw = np.asarray(inputs["L0_ffn_qweights"], dtype=np.int32)
    scl = np.asarray(inputs["L0_ffn_scales"], dtype=np.float32)
    bias = np.asarray(inputs["L0_ffn_biases"], dtype=np.float32)
    la = np.asarray(inputs["L0_ffn_lora_a"], dtype=np.float32)
    lb = np.asarray(inputs["L0_ffn_lora_b"], dtype=np.float32)
    gam = np.asarray(inputs["L0_ffn_ln_gamma"], dtype=np.float32)
    bet = np.asarray(inputs["L0_ffn_ln_beta"], dtype=np.float32)

    shared = {}
    shared["wqkvT"] = np.ascontiguousarray(Wqkv.T)
    shared["wprojT"] = np.ascontiguousarray(Wproj.T)
    shared["qT"] = np.ascontiguousarray(qw.transpose(0, 2, 1))
    shared["sclT"] = np.ascontiguousarray(scl.transpose(0, 2, 1))
    shared["laT"] = np.ascontiguousarray(la)
    shared["lbT"] = np.ascontiguousarray(lb.transpose(0, 2, 1))

    sm = np.zeros((128, SM_N), dtype=np.float32)
    sm[:, SM_BQ:SM_BQ + 8] = bqkv[0:1024].reshape(8, 128).T
    sm[:, SM_BK:SM_BK + 8] = bqkv[1024:2048].reshape(8, 128).T
    sm[:, SM_BPROJ:SM_BPROJ + 8] = bproj.reshape(8, 128).T
    for j in range(6):
        sm[:, SM_BIAS + j * 8: SM_BIAS + (j + 1) * 8] = bias[j].reshape(8, 128).T
    for blk in range(3):
        sm[:, SM_GAMMA + blk * 8: SM_GAMMA + (blk + 1) * 8] = gam[blk].reshape(8, 128).T
        sm[:, SM_BETA + blk * 8: SM_BETA + (blk + 1) * 8] = bet[blk].reshape(8, 128).T
    sm[:, SM_EPS] = 1e-5
    sm[:, SM_BV:SM_BV + 8] = bqkv[2048:3072].reshape(8, 128).T
    shared["smalls"] = sm

    co = np.zeros((128, CO_N), dtype=BF)
    co[:, CO_ONES:CO_ONES + 128] = 1.0
    A33 = np.zeros((33, 128), dtype=np.float32)
    P33 = np.zeros((33, 128), dtype=np.float32)
    r = np.arange(128)
    A33[0, :] = -128.0
    P33[0, :] = 128.0
    for b in range(16):
        inb = (r // 8 == b)
        A33[1 + 2 * b, :] = 128.0 * inb
        P33[1 + 2 * b, :] = 128.0 * inb
        A33[2 + 2 * b, :] = -128.0 * (inb & (r % 8 >= 4))
        P33[2 + 2 * b, :] = 128.0 * (inb & (r % 8 < 4))
    co[0:33, CO_A33:CO_A33 + 128] = A33.astype(BF)
    co[0:33, CO_MV33:CO_MV33 + 512] = np.tile(P33, (1, 4)).astype(BF)
    for c in range(NCH):
        E = np.zeros((64, 128), dtype=np.float32)
        d = np.arange(128)
        E[(8 * c + d // 16), d] = 1.0
        co[0:64, CO_ESEL + c * 128: CO_ESEL + (c + 1) * 128] = E.astype(BF)
    shared["consts"] = co

    per_core = []
    for core in range(NCORES):
        xs = x[:, core * BLOC:(core + 1) * BLOC, :]
        xtc = np.ascontiguousarray(xs.transpose(2, 1, 0).reshape(D, T))
        per_core.append({"xt": xtc})
    return shared, per_core


def kernel(**inputs):
    shared, per_core = _host_pack(inputs)
    nc = build_nc()
    in_maps = [{**shared, **pc} for pc in per_core]
    res = run_bass_kernel_spmd(nc, in_maps, core_ids=list(range(NCORES)))
    outs = []
    for core in range(NCORES):
        o = np.asarray(res.results[core]["o"])
        outs.append(o.reshape(D, BLOC, SEQN).transpose(2, 1, 0))
    return np.concatenate(outs, axis=1).astype(np.float32)
